# version 02 (final)
# baseline (speedup 1.0000x reference)
import math
from contextlib import ExitStack

import numpy as np
import ml_dtypes

import concourse.bass as bass
import concourse.mybir as mybir
from concourse.bass_utils import run_bass_kernel_spmd

F32 = mybir.dt.float32
BF16 = mybir.dt.bfloat16
AF = mybir.ActivationFunctionType
ALU = mybir.AluOpType
AX = mybir.AxisListType
BF = ml_dtypes.bfloat16

D_MODEL = 2048
D_INNER = 4096
HEADDIM = 64
INTER = 8192
CHUNK = 256
NCHUNK = 4
T = 1024
EPS = 1e-5
SC2048 = 1.0 / math.sqrt(2048.0)
SC8192 = 1.0 / math.sqrt(8192.0)

NZ = 32
NXBC = 34
NDIP = 67
NDM = 16
NFI = 32
NA = 64
PITCH = 1056
KB = 1024


def _had(n):
    h = np.array([[1.0]], dtype=np.float32)
    while h.shape[0] < n:
        h = np.block([[h, h], [h, -h]])
    return h


def _stage_shared(inputs):
    d = {}
    w = np.asarray(inputs["L0_attn_in_proj_w"])
    wp = np.zeros((NDIP * 128, D_MODEL), np.float32)
    wp[:8512] = w
    wt = np.ascontiguousarray(wp.T)
    d["w_in"] = np.ascontiguousarray(
        wt.reshape(NDM, 128, NDIP, 128).transpose(2, 1, 0, 3))

    q = np.arange(128)
    j = np.arange(16)
    fmap = (q[None, :] % 16) * 128 + 16 * (q[None, :] // 16) + j[:, None]
    wu = np.asarray(inputs["L0_ffn_W_up"]).reshape(NA, 128, D_MODEL)
    wg = np.asarray(inputs["L0_ffn_W_gate"]).reshape(NA, 128, D_MODEL)
    d["w_up"] = np.ascontiguousarray(wu[:, :, fmap].transpose(0, 3, 2, 1))
    d["w_gate"] = np.ascontiguousarray(wg[:, :, fmap].transpose(0, 3, 2, 1))

    wd = np.asarray(inputs["L0_ffn_W_down"])
    g = np.arange(NA)
    p = np.arange(128)
    f2map = (p[None, :] % 64) * 128 + 64 * (p[None, :] // 64) + g[:, None]
    d["w_down"] = np.ascontiguousarray(wd[:, f2map].transpose(1, 2, 0))

    wo = np.asarray(inputs["L0_attn_out_proj_w"])
    d["w_out"] = np.ascontiguousarray(wo.T.reshape(NFI, 128, D_MODEL))

    d["ident"] = np.eye(128, dtype=BF)
    d["identf"] = np.eye(128, dtype=np.float32)
    d["h128"] = _had(128).astype(BF)
    d["h16blk"] = np.kron(np.eye(8), _had(16)).astype(BF)
    d["h64blk"] = np.kron(np.eye(2), _had(64)).astype(BF)
    s = np.arange(CHUNK)[:, None]
    c = np.arange(CHUNK)[None, :]
    m = (s <= c).astype(BF)
    d["mask01"] = np.ascontiguousarray(
        np.concatenate([m[0:128], m[128:256]], axis=1))
    d["ones_f"] = np.ones((1, 128), np.float32)
    d["ones_bf"] = np.ones((1, 128), BF)
    d["ones_cb"] = np.ones((128, 1), BF)

    f32 = np.float32
    d["dt_bias"] = np.asarray(inputs["L0_attn_dt_bias"]).astype(f32).reshape(64, 1)
    d["a_log"] = np.asarray(inputs["L0_attn_A_log"]).astype(f32).reshape(64, 1)
    d["d_rep"] = np.ascontiguousarray(
        np.repeat(np.asarray(inputs["L0_attn_D"]).astype(f32), HEADDIM)
        .reshape(NFI, 128).T)
    d["nw"] = np.ascontiguousarray(
        np.asarray(inputs["L0_attn_norm_weight"]).astype(f32).reshape(NFI, 128).T)
    d["conv_w"] = np.ascontiguousarray(
        np.asarray(inputs["L0_attn_conv_w"]).astype(f32)
        .reshape(NXBC, 128, 4).transpose(1, 0, 2).reshape(128, NXBC * 4))
    d["conv_b"] = np.ascontiguousarray(
        np.asarray(inputs["L0_attn_conv_b"]).astype(f32).reshape(NXBC, 128).T)
    d["su_ug"] = np.ascontiguousarray(
        np.asarray(inputs["L0_ffn_SU_ug"]).astype(f32).reshape(NDM, 128).T)
    d["su_dp"] = np.ascontiguousarray(
        np.asarray(inputs["L0_ffn_SU_dp"]).astype(f32).reshape(NA, 128).T)
    wsug = np.asarray(inputs["L0_ffn_Wscale_ug"]).astype(f32)
    d["wsc_up"] = np.ascontiguousarray(wsug[:INTER].reshape(NA, 128).T)
    d["wsc_gate"] = np.ascontiguousarray(wsug[INTER:].reshape(NA, 128).T)
    d["wsc_dp"] = np.asarray(inputs["L0_ffn_Wscale_dp"]).astype(f32).reshape(4, 512)
    return d


def _stage_core(inputs, core):
    b, half = core // 2, core % 2
    x = np.asarray(inputs["L0_attn_u"])
    d = {}
    d["x_rows"] = np.ascontiguousarray(x[b, half * T:(half + 1) * T, :])
    if half == 1:
        d["x_prev3"] = np.ascontiguousarray(x[b, T - 3:T, :])
        d["x_carry"] = np.ascontiguousarray(x[b, 0:T, :])
        d["gate"] = np.ones((128, 1), np.float32)
    else:
        d["x_prev3"] = np.zeros((3, D_MODEL), np.float32)
        d["x_carry"] = np.zeros((T, D_MODEL), np.float32)
        d["gate"] = np.zeros((128, 1), np.float32)
    d["zeros3"] = np.zeros((3, D_MODEL), np.float32)
    return d


class Sem:
    def __init__(self, nc, ctx, name):
        self.h = ctx.enter_context(nc.semaphore(name))
        self.n = 0

    def inc(self, instr, k=1):
        instr.then_inc(self.h, k)
        self.n += k
        return self.n


class _TeWrap:

    def __init__(self, eng):
        self._e = eng
        self._last = None

    def matmul(self, *a, **k):
        self._last = self._e.matmul(*a, **k)
        return self._last

    def transpose(self, *a, **k):
        self._last = self._e.transpose(*a, **k)
        return self._last

    def wait_ge(self, *a, **k):
        return self._e.wait_ge(*a, **k)

    def nop(self):
        assert self._last is not None
        last, self._last = self._last, None
        return last


def build_nc(tap=None, stop_after=None):
    nc = bass.Bass("TRN2", target_bir_lowering=False, detect_race_conditions=False)
    ctx = ExitStack()
    te, ve, se, ge, qe = _TeWrap(nc.tensor), nc.vector, nc.scalar, nc.gpsimd, nc.sync

    def din(name, shape, dtype=F32):
        return nc.dram_tensor(name, list(shape), dtype, kind="ExternalInput")

    x_rows = din("x_rows", [T, D_MODEL])
    x_prev3 = din("x_prev3", [3, D_MODEL])
    x_carry = din("x_carry", [T, D_MODEL])
    zeros3 = din("zeros3", [3, D_MODEL])
    gate_in = din("gate", [128, 1])
    w_in = din("w_in", [NDIP, 128, NDM, 128])
    w_up = din("w_up", [NA, 128, NDM, 128])
    w_gate = din("w_gate", [NA, 128, NDM, 128])
    w_down = din("w_down", [NA, 128, D_MODEL])
    w_out = din("w_out", [NFI, 128, D_MODEL])
    ident = din("ident", [128, 128], BF16)
    identf = din("identf", [128, 128], F32)
    h128 = din("h128", [128, 128], BF16)
    h16blk = din("h16blk", [128, 128], BF16)
    h64blk = din("h64blk", [128, 128], BF16)
    mask01 = din("mask01", [128, 512], BF16)
    ones_f = din("ones_f", [1, 128])
    ones_bf = din("ones_bf", [1, 128], BF16)
    ones_cb = din("ones_cb", [128, 1], BF16)
    dt_bias = din("dt_bias", [64, 1])
    a_log = din("a_log", [64, 1])
    d_rep = din("d_rep", [128, NFI])
    nw_in = din("nw", [128, NFI])
    conv_w = din("conv_w", [128, NXBC * 4])
    conv_b = din("conv_b", [128, NXBC])
    su_ug = din("su_ug", [128, NDM])
    su_dp = din("su_dp", [128, NA])
    wsc_up = din("wsc_up", [128, NA])
    wsc_gate = din("wsc_gate", [128, NA])
    wsc_dp = din("wsc_dp", [4, 512])

    out_t = nc.dram_tensor("out", [T, D_MODEL], F32, kind="ExternalOutput")
    tap_t = None
    if tap is not None:
        tap_t = nc.dram_tensor("tap", list(tap[1]),
                               tap[2] if len(tap) > 2 else F32,
                               kind="ExternalOutput")

    z_hbm = nc.dram_tensor("z_hbm", [NZ, 128, T], BF16)
    s_hbm = nc.dram_tensor("s_hbm", [8, 128, D_MODEL], BF16)
    cs_hbm = nc.dram_tensor("cs_hbm", [64, CHUNK], F32)
    xn_hbm = nc.dram_tensor("xn_hbm", [NDM, 128, T], BF16)

    ps = [ctx.enter_context(nc.psum_tensor(f"ps{i}", [128, 512], F32))
          for i in range(5)]
    pt = [ctx.enter_context(nc.psum_tensor(f"pt{i}", [128, 1024], BF16))
          for i in range(2)]
    ps_x = ctx.enter_context(nc.psum_tensor("ps_x", [128, 512], F32))

    sQs = Sem(nc, ctx, "sQs")
    sQa = Sem(nc, ctx, "sQa")
    sQg = Sem(nc, ctx, "sQg")
    sPE = Sem(nc, ctx, "sPE")
    sV = Sem(nc, ctx, "sV")
    sS = Sem(nc, ctx, "sS")
    sG = Sem(nc, ctx, "sG")

    LIMIT = 212832
    _arena = nc.alloc_sbuf_tensor("arena_reserved", [128, LIMIT], mybir.dt.uint8)
    ABASE = nc.lookup_mloc(_arena).addr
    off = [0]

    def alloc(name, shape, dtype, at=None):
        per = int(np.prod(shape[1:])) * mybir.dt.size(dtype)
        if at is None:
            a = off[0]
            off[0] = (a + per + 31) // 32 * 32
        else:
            a = at
        assert a + per <= LIMIT, f"SBUF overflow: {name} @{a}+{per}"
        return nc.alloc_sbuf_tensor_at(name, list(shape), dtype, offset=ABASE + a)

    c_ident = alloc("c_ident", [128, 128], BF16)
    c_identf = alloc("c_identf", [128, 128], F32)
    c_h128 = alloc("c_h128", [128, 128], BF16)
    c_h16 = alloc("c_h16", [128, 128], BF16)
    c_h64 = alloc("c_h64", [128, 128], BF16)
    c_mask = alloc("c_mask", [128, 512], BF16)
    c_onesf = alloc("c_onesf", [1, 128], F32)
    c_onesb = alloc("c_onesb", [1, 128], BF16)
    c_onescb = alloc("c_onescb", [128, 1], BF16)
    c_dtb = alloc("c_dtb", [64, 1], F32)
    c_alog = alloc("c_alog", [64, 1], F32)
    c_aneg = alloc("c_aneg", [64, 1], F32)
    c_gate = alloc("c_gate", [128, 1], F32)
    c_drep = alloc("c_drep", [128, NFI], F32)
    c_nw = alloc("c_nw", [128, NFI], F32)
    c_convw = alloc("c_convw", [128, NXBC * 4], F32)
    c_convb = alloc("c_convb", [128, NXBC], F32)
    c_suug = alloc("c_suug", [128, NDM], F32)
    c_sudp = alloc("c_sudp", [128, NA], F32)
    c_wscup = alloc("c_wscup", [128, NA], F32)
    c_wscg = alloc("c_wscg", [128, NA], F32)
    dtc = alloc("dtc", [64, CHUNK], F32)
    csc = alloc("csc", [64, CHUNK], F32)
    dec = alloc("dec", [64, CHUNK], F32)
    cstT = alloc("cstT", [128, 128], F32)
    dtT = alloc("dtT", [128, 128], F32)
    decT = alloc("decT", [128, 128], F32)
    dbc = alloc("dbc", [128, 64], F32)
    cd_row = alloc("cd_row", [1, 64], F32)
    rsq_r = alloc("rsq_r", [1, CHUNK], F32)
    hcol = alloc("hcol", [128, NA], F32)
    c_ones64 = alloc("c_ones64", [64, 128], BF16)
    cdcol = alloc("cdcol", [64, 1], F32)
    cdcolb = alloc("cdcolb", [64, 1], BF16)
    cdd = alloc("cdd", [64, 64], BF16)
    c_eps = alloc("c_eps", [128, 1], F32)
    cshi = alloc("cshi", [64, CHUNK], BF16)
    cslo = alloc("cslo", [64, CHUNK], BF16)
    dtb16 = alloc("dtb16", [64, CHUNK], BF16)
    decb16 = alloc("decb16", [64, CHUNK], BF16)
    c_one = alloc("c_one", [128, 1], F32)
    c_zero = alloc("c_zero", [128, 1], F32)
    wstg = [alloc(f"wstg{i}", [128, 2048], F32) for i in range(2)]
    wbf = [alloc(f"wbf{i}", [128, 2048], BF16) for i in range(2)]
    convd = [alloc(f"convd{i}", [128, 512], BF16) for i in range(2)]
    ddtile = [alloc(f"ddtile{i}", [128, 128], BF16) for i in range(2)]
    stcast = [alloc(f"stcast{i}", [128, 128], BF16) for i in range(2)]
    hbtile = [alloc(f"hbtile{i}", [128, 128], BF16) for i in range(2)]
    sgbuf = [alloc(f"sgbuf{i}", [128, 512], BF16) for i in range(2)]
    vbuf = [alloc(f"vbuf{i}", [128, 512], BF16) for i in range(2)]
    PERS = off[0]
    assert PERS <= 46 * KB, PERS

    R1 = 46 * KB
    R2 = 80896
    R3 = 144 * KB
    xn_fm = alloc("xn_fm", [128, NDM * PITCH], BF16, at=R1)
    RBb = alloc("RBb", [128, 2048], F32, at=R1)
    Ebuf = alloc("Ebuf", [128, 2 * 2048], BF16, at=R1 + 8 * KB)
    Xdt = alloc("Xdt", [128, 2 * D_INNER], BF16, at=R1 + 16 * KB)
    s_stage = [alloc(f"s_stage{i}", [128, 512], BF16, at=R1 + i * KB)
               for i in range(2)]
    h1perm = alloc("h1perm", [128, 16 * 1024], BF16, at=R1)
    wscRB = alloc("wscRB", [128, D_MODEL], F32, at=R1)
    wdp_row = alloc("wdp_row", [1, 512], F32, at=R1 + 8 * KB)
    gp_f = alloc("gp_f", [128, 512], F32, at=R1 + 10 * KB)
    gp_b = [alloc(f"gp_b{i}", [128, 512], BF16, at=R1 + 12 * KB + i * KB)
            for i in range(2)]
    sp_b = [alloc(f"sp_b{i}", [128, 512], BF16, at=R1 + 14 * KB + i * KB)
            for i in range(2)]
    xpf = [alloc(f"xpf{i}", [128, 512], F32, at=R1 + 16 * KB + i * 2 * KB)
           for i in range(2)]
    opf = [alloc(f"opf{i}", [128, 512], F32, at=R1 + 20 * KB + i * 2 * KB)
           for i in range(2)]
    x_ssm = alloc("x_ssm", [128, NFI * 1024], BF16, at=R2)
    H2buf = alloc("H2buf", [128, NA * 1024], BF16, at=R2)
    state = alloc("state", [128, D_INNER], F32, at=R3 + 16 * KB)
    dt_raw = alloc("dt_raw", [64, T], F32, at=R3 + 56 * KB)
    xload = [alloc(f"xload{i}", [128, D_MODEL], F32, at=R3 + i * 8 * KB)
             for i in range(2)]
    xn_tm = [alloc(f"xn_tm{i}", [128, D_MODEL], BF16, at=R3 + 32 * KB + i * 4 * KB)
             for i in range(2)]
    sqd = alloc("sqd", [128, D_MODEL], F32, at=R3 + 40 * KB)
    xload.append(alloc("xload2", [128, D_MODEL], F32, at=R2))
    stm = [alloc(f"stm{i}", [128, 8], F32, at=R3 + 52 * KB + i * 64)
           for i in range(2)]
    stv = [alloc(f"stv{i}", [128, 8], F32, at=R3 + 52 * KB + 128 + i * 64)
           for i in range(2)]
    st2 = [alloc(f"st2{i}", [128, 8], F32, at=R3 + 52 * KB + 256 + i * 64)
           for i in range(2)]
    zstage = [alloc(f"zstage{i}", [128, T], BF16, at=R3 + i * 2 * KB)
              for i in range(2)]
    xbst = [alloc(f"xbst{i}", [128, PITCH], BF16, at=R3 + 4 * KB + i * 2176)
            for i in range(2)]
    Bfm = alloc("Bfm", [128, T], BF16, at=R3 + 10 * KB)
    Cfm = alloc("Cfm", [128, T], BF16, at=R3 + 12 * KB)
    Tbuf = alloc("Tbuf", [128, 2048], F32, at=R3)
    Gm = alloc("Gm", [128, 2 * CHUNK], BF16, at=R3 + 8 * KB)
    Btm = alloc("Btm", [128, 2 * 128], BF16, at=R3 + 9 * KB)
    ySq = alloc("ySq", [128, CHUNK], BF16, at=R3 + 9 * KB + 512)
    szbuf = alloc("szbuf", [128, NZ * CHUNK], BF16, at=R3 + 32 * KB)
    ExpRB = alloc("ExpRB", [128, 2048], BF16, at=R3 + 48 * KB)
    Cexp = [alloc("Cexp0", [128, 2048], BF16, at=R3 + 52 * KB)]
    xnld = [alloc(f"xnld{i}", [128, 1024], BF16, at=R3 + i * 2 * KB)
            for i in range(2)]
    v1a = [alloc(f"v1a{i}", [128, 1024], BF16, at=R3 + 4 * KB + i * 2 * KB)
           for i in range(2)]

    bar = nc.all_engine_barrier
    ST = {}
    _stopped = [False]

    def ckpt(name):
        if stop_after == name and not _stopped[0]:
            _stopped[0] = True
            qe.wait_ge(sV.h, sV.n)
            qe.wait_ge(sS.h, sS.n)
            qe.wait_ge(sPE.h, sPE.n)
            qe.wait_ge(sG.h, sG.n)
            qe.wait_ge(sQg.h, sQg.n)
            qe.wait_ge(sQa.h, sQa.n)
            sQs.inc(qe.dma_start(out=out_t.ap(), in_=x_rows.ap()), 16)
            qe.wait_ge(sQs.h, sQs.n)
        return _stopped[0]

    def tap_now(name, src_ap):
        if tap is not None and tap[0] == name:
            qe.wait_ge(sV.h, sV.n)
            qe.wait_ge(sS.h, sS.n)
            sQs.inc(qe.dma_start(out=tap_t.ap(), in_=src_ap), 16)
            qe.wait_ge(sQs.h, sQs.n)

    for dst, src in ((c_ident, ident), (c_identf, identf), (c_h128, h128),
                     (c_h16, h16blk), (c_h64, h64blk), (c_mask, mask01),
                     (c_onesf, ones_f), (c_onesb, ones_bf),
                     (c_onescb, ones_cb),
                     (c_dtb, dt_bias), (c_alog, a_log), (c_gate, gate_in),
                     (c_drep, d_rep), (c_nw, nw_in), (c_convw, conv_w),
                     (c_convb, conv_b), (c_suug, su_ug), (c_sudp, su_dp),
                     (c_wscup, wsc_up), (c_wscg, wsc_gate)):
        idx = tuple(slice(None) for _ in dst.shape)
        sQs.inc(qe.dma_start(out=dst[idx], in_=src.ap()), 16)
    ge.wait_ge(sQs.h, sQs.n)
    sG.inc(ge.memset(c_eps[:, :], EPS))
    sG.inc(ge.memset(c_one[:, :], 1.0))
    sG.inc(ge.memset(c_zero[:, :], 0.0))
    sG.inc(ge.memset(c_ones64[:, :], 1.0))
    se.wait_ge(sQs.h, sQs.n)
    sS.inc(se.activation(c_aneg[:, :], c_alog[:, :], AF.Exp))
    ve.wait_ge(sS.h, sS.n)
    sV.inc(ve.tensor_scalar_mul(c_aneg[:, :], c_aneg[:, :], -1.0))
    sV.inc(ve.tensor_scalar_mul(c_wscg[:, :], c_wscg[:, :], SC2048))
    sV.inc(ve.tensor_mul(hcol[:, :], c_sudp[:, :], c_wscup[:, :]))
    sV.inc(ve.tensor_scalar_mul(hcol[:, :], hcol[:, :], SC2048))
    bar()
    if ckpt("P0"):
        ctx.close()
        return nc

    def emit_ln(x_main, x_p3, spill_xn):
        apply_done = {}
        tr_done = {}
        dma_n = {}
        var_done = {}
        sqrt_gate = {}
        sqrt_done = {}

        def rw(r):
            return 3 if r == 0 else 128

        def ln_load(r):
            src = x_p3.ap() if r == 0 else x_main.ap()[(r - 1) * 128:r * 128, :]
            if r >= 3:
                qe.wait_ge(sV.h, apply_done[r - 3])
            sQs.inc(qe.dma_start(out=xload[r % 3][0:rw(r), :], in_=src), 16)
            dma_n[r] = sQs.n

        def ln_apply(r):
            rows = rw(r)
            p = r % 2
            j = r // 2
            if r >= 2:
                ve.wait_ge(sPE.h, tr_done[r - 2])
            sV.inc(ve.tensor_scalar(xn_tm[r % 2][0:rows, :],
                                    xload[r % 3][0:rows, :],
                                    stm[p][0:rows, j:j + 1],
                                    stv[p][0:rows, j:j + 1],
                                    ALU.subtract, ALU.mult))
            apply_done[r] = sV.n

        def ln_waves(r):
            rows = rw(r)
            col = 0 if r == 0 else 3 + (r - 1) * 128
            for wv in range(8):
                te.wait_ge(sV.h, apply_done[r] if wv == 0 else sV.n)
                for jj in range(2):
                    jch = wv * 2 + jj
                    te.transpose(pt[jj][:, 0:rows],
                                 xn_tm[r % 2][0:rows, jch * 128:(jch + 1) * 128],
                                 c_ident[0:rows, 0:rows])
                sPE.inc(te.nop())
                ve.wait_ge(sPE.h, sPE.n)
                for jj in range(2):
                    jch = wv * 2 + jj
                    sV.inc(ve.tensor_copy(
                        xn_fm[:, jch * PITCH + col: jch * PITCH + col + rows],
                        pt[jj][:, 0:rows]))
                te.wait_ge(sV.h, sV.n)
            tr_done[r] = sPE.n

        def ln_sqrt(r):
            rows = rw(r)
            p = r % 2
            j = r // 2
            se.wait_ge(sV.h, sqrt_gate[r])
            sS.inc(se.activation(stv[p][0:rows, j:j + 1],
                                 stv[p][0:rows, j:j + 1],
                                 AF.Sqrt, bias=c_eps[0:rows, 0:1]))
            sqrt_done[r] = sS.n

        def ln_recip(r):
            rows = rw(r)
            p = r % 2
            j = r // 2
            ve.wait_ge(sS.h, sqrt_done[r])
            ve.reciprocal(stv[p][0:rows, j:j + 1], stv[p][0:rows, j:j + 1])

        def ln_iter(r):
            rows = rw(r)
            p = r % 2
            j = r // 2
            xt = xload[r % 3][0:rows, :]
            ve.wait_ge(sQs.h, dma_n[r])
            ve.reduce_sum(stm[p][0:rows, j:j + 1], xt, AX.X)
            ve.scalar_tensor_tensor(sqd[0:rows, :], xt, 1.0, xt,
                                    ALU.bypass, ALU.mult)
            ve.reduce_sum(stv[p][0:rows, j:j + 1], sqd[0:rows, :], AX.X)
            ve.scalar_tensor_tensor(st2[p][0:rows, j:j + 1],
                                    stm[p][0:rows, j:j + 1],
                                    1.0 / (2048.0 * 2048.0),
                                    stm[p][0:rows, j:j + 1],
                                    ALU.mult, ALU.mult)
            ve.tensor_scalar_mul(stm[p][0:rows, j:j + 1],
                                 stm[p][0:rows, j:j + 1], 1.0 / D_MODEL)
            if r >= 1:
                ln_recip(r - 1)
            if r >= 2:
                ln_apply(r - 2)
            else:
                ve.scalar_tensor_tensor(sqd[0:rows, :], xt, 1.0, xt,
                                        ALU.bypass, ALU.mult)
            sV.inc(ve.scalar_tensor_tensor(
                stv[p][0:rows, j:j + 1], stv[p][0:rows, j:j + 1],
                1.0 / D_MODEL, st2[p][0:rows, j:j + 1],
                ALU.mult, ALU.subtract))
            var_done[r] = sV.n
            if r >= 2 and rw(r - 2) == 128:
                ln_waves(r - 2)
                sqrt_gate[r] = sV.n
            else:
                if r >= 2:
                    ln_waves(r - 2)
                sV.inc(ve.scalar_tensor_tensor(sqd[0:rows, :], xt, 1.0, xt,
                                               ALU.bypass, ALU.mult))
                sqrt_gate[r] = sV.n
            ln_sqrt(r)

        for r in range(9):
            ln_load(r)
            ln_iter(r)
        ln_recip(8)
        ln_apply(7)
        ln_waves(7)
        ln_apply(8)
        ln_waves(8)
        if spill_xn:
            ge.wait_ge(sV.h, sV.n)
            for jch in range(NDM):
                sQg.inc(ge.dma_start(
                    out=xn_hbm.ap()[jch],
                    in_=xn_fm[:, jch * PITCH + 3: jch * PITCH + 1027]), 16)

    def emit_inproj(d_chunks, spill_z):
        pe_done = {}
        cast_done = {}
        evac_done = {}
        zdma_done = {}
        conv_done = {}
        silu_done = {}
        nxb = 0
        nz = 0
        prev_xbc_i = -1
        for i, d in enumerate(d_chunks):
            par = i % 2
            if i >= 2:
                qe.wait_ge(sG.h, cast_done[i - 2][0])
                qe.wait_ge(sS.h, cast_done[i - 2][1])
            sQs.inc(qe.dma_start(out=wstg[par][:, :],
                                 in_=w_in.ap()[d].rearrange("p k m -> p (k m)")), 16)
            slab_n = sQs.n
            is_xbc = 32 <= d <= 65
            jx = d - 32
            if is_xbc:
                if nxb >= 2:
                    ve.wait_ge(sPE.h, conv_done[nxb - 2])
                for k in range(4):
                    sV.inc(ve.tensor_scalar_mul(
                        convd[nxb % 2][:, k * 128:(k + 1) * 128], c_ident[:, :],
                        c_convw[:, jx * 4 + k:jx * 4 + k + 1]))
            if i >= 2:
                ge.wait_ge(sPE.h, pe_done[i - 2])
                se.wait_ge(sPE.h, pe_done[i - 2])
            ge.wait_ge(sQs.h, slab_n)
            sG.inc(ge.tensor_copy(wbf[par][:, 0:1024], wstg[par][:, 0:1024]))
            se.wait_ge(sQs.h, slab_n)
            sS.inc(se.activation(wbf[par][:, 1024:2048], wstg[par][:, 1024:2048],
                                 AF.Copy))
            cast_done[i] = (sG.n, sS.n)
            te.wait_ge(sG.h, sG.n)
            te.wait_ge(sS.h, sS.n)
            if i >= 2:
                te.wait_ge(sV.h, evac_done[i - 2])
            pa, pb = ps[2 * par], ps[2 * par + 1]
            for k in range(NDM):
                lh = wbf[par][:, k * 128:(k + 1) * 128]
                te.matmul(pa[:, :], lh, xn_fm[:, k * PITCH + 3:k * PITCH + 515],
                          start=(k == 0), stop=(k == NDM - 1))
                te.matmul(pb[:, :], lh, xn_fm[:, k * PITCH + 515:k * PITCH + 1027],
                          start=(k == 0), stop=(k == NDM - 1))
            if is_xbc:
                if prev_xbc_i >= 0:
                    te.wait_ge(sV.h, evac_done[prev_xbc_i])
                    te.wait_ge(sS.h, silu_done[nxb - 1])
                for k in range(NDM):
                    te.matmul(ps_x[:, 0:3], wbf[par][:, k * 128:(k + 1) * 128],
                              xn_fm[:, k * PITCH:k * PITCH + 3],
                              start=(k == 0), stop=(k == NDM - 1))
            pe_done[i] = sPE.inc(te.nop())
            ve.wait_ge(sPE.h, sPE.n)
            if d <= 31:
                if nz >= 2 and spill_z:
                    ve.wait_ge(sQg.h, zdma_done[nz - 2])
                sV.inc(ve.tensor_copy(zstage[nz % 2][:, 0:512], pa[:, :]))
                sV.inc(ve.tensor_copy(zstage[nz % 2][:, 512:1024], pb[:, :]))
                evac_done[i] = sV.n
                if spill_z:
                    ge.wait_ge(sV.h, sV.n)
                    sQg.inc(ge.dma_start(out=z_hbm.ap()[d],
                                         in_=zstage[nz % 2][:, :]), 16)
                    zdma_done[nz] = sQg.n
                nz += 1
            elif d == 66:
                sV.inc(ve.tensor_copy(dt_raw[:, 0:512], pa[0:64, :]))
                sV.inc(ve.tensor_copy(dt_raw[:, 512:1024], pb[0:64, :]))
                evac_done[i] = sV.n
            else:
                st = xbst[nxb % 2]
                if nxb >= 2:
                    ve.wait_ge(sPE.h, conv_done[nxb - 2])
                sV.inc(ve.tensor_copy(st[:, 0:3], ps_x[:, 0:3]))
                sV.inc(ve.tensor_copy(st[:, 3:515], pa[:, :]))
                sV.inc(ve.tensor_copy(st[:, 515:1027], pb[:, :]))
                evac_done[i] = sV.n
                te.wait_ge(sV.h, sV.n)
                if nxb >= 1:
                    te.wait_ge(sS.h, silu_done[nxb - 1])
                for k in range(4):
                    cdg = convd[nxb % 2][:, k * 128:(k + 1) * 128]
                    te.matmul(ps[4][:, :], cdg, st[:, k:k + 512],
                              start=(k == 0), stop=(k == 3))
                    te.matmul(ps_x[:, :], cdg, st[:, 512 + k:512 + k + 512],
                              start=(k == 0), stop=(k == 3))
                conv_done[nxb] = sPE.inc(te.nop())
                se.wait_ge(sPE.h, sPE.n)
                if jx < 32:
                    dsts = (x_ssm[:, jx * 1024:jx * 1024 + 512],
                            x_ssm[:, jx * 1024 + 512:(jx + 1) * 1024])
                elif jx == 32:
                    dsts = (Bfm[:, 0:512], Bfm[:, 512:1024])
                else:
                    dsts = (Cfm[:, 0:512], Cfm[:, 512:1024])
                sS.inc(se.activation(dsts[0], ps[4][:, :], AF.Silu,
                                     bias=c_convb[:, jx:jx + 1]))
                sS.inc(se.activation(dsts[1], ps_x[:, :], AF.Silu,
                                     bias=c_convb[:, jx:jx + 1]))
                silu_done[nxb] = sS.n
                prev_xbc_i = i
                nxb += 1

    def emit_dt_cs(c):
        sl = slice(c * CHUNK, (c + 1) * CHUNK)
        se.wait_ge(sV.h, sV.n)
        sS.inc(se.activation(dtc[:, :], dt_raw[:, sl], AF.Exp, bias=c_dtb[:, :]))
        sS.inc(se.activation(dtc[:, :], dtc[:, :], AF.Ln, bias=c_one[0:64, 0:1]))
        ve.wait_ge(sS.h, sS.n)
        if "cs_w" in ST:
            ve.wait_ge(sQs.h, ST["cs_w"])
        sV.inc(ve.tensor_scalar_mul(csc[:, :], dtc[:, :], c_aneg[:, :]))
        sV.inc(ve.tensor_tensor_scan(csc[:, :], csc[:, :], dtc[:, :],
                                     c_zero[0:64, 0:1], ALU.add, ALU.bypass))
        sV.inc(ve.tensor_copy(cshi[:, :], csc[:, :]))
        sV.inc(ve.tensor_sub(cslo[:, :], csc[:, :], cshi[:, :]))
        sV.inc(ve.tensor_copy(dtb16[:, :], dtc[:, :]))
        for stile in range(2):
            te.wait_ge(sV.h, sV.n)
            te.transpose(pt[0][:, 0:64], cshi[:, stile * 128:(stile + 1) * 128],
                         c_ident[0:64, 0:64])
            te.transpose(pt[1][:, 0:64], cslo[:, stile * 128:(stile + 1) * 128],
                         c_ident[0:64, 0:64])
            sPE.inc(te.nop())
            ve.wait_ge(sPE.h, sPE.n)
            ve.tensor_copy(cstT[:, stile * 64:(stile + 1) * 64], pt[0][:, 0:64])
            sV.inc(ve.tensor_add(cstT[:, stile * 64:(stile + 1) * 64],
                                 cstT[:, stile * 64:(stile + 1) * 64],
                                 pt[1][:, 0:64]))
            te.wait_ge(sV.h, sV.n)
            te.transpose(pt[0][:, 0:64], dtb16[:, stile * 128:(stile + 1) * 128],
                         c_ident[0:64, 0:64])
            sPE.inc(te.nop())
            ve.wait_ge(sPE.h, sPE.n)
            sV.inc(ve.tensor_copy(dtT[:, stile * 64:(stile + 1) * 64],
                                  pt[0][:, 0:64]))
            te.wait_ge(sV.h, sV.n)
        se.wait_ge(sV.h, sV.n)
        sS.inc(se.activation(dec[:, :], csc[:, :], AF.Exp, scale=-1.0,
                             bias=csc[:, CHUNK - 1:CHUNK]))
        sS.inc(se.activation(cdcol[:, :], csc[:, CHUNK - 1:CHUNK], AF.Exp))
        ve.wait_ge(sS.h, sS.n)
        sV.inc(ve.tensor_copy(decb16[:, :], dec[:, :]))
        for stile in range(2):
            te.wait_ge(sV.h, sV.n)
            te.transpose(pt[1][:, 0:64], decb16[:, stile * 128:(stile + 1) * 128],
                         c_ident[0:64, 0:64])
            sPE.inc(te.nop())
            ve.wait_ge(sPE.h, sPE.n)
            sV.inc(ve.tensor_copy(decT[:, stile * 64:(stile + 1) * 64],
                                  pt[1][:, 0:64]))
            te.wait_ge(sV.h, sV.n)
        sV.inc(ve.tensor_scalar_mul(cdd[:, :], c_ident[0:64, 0:64], cdcol[:, :]))
        te.wait_ge(sV.h, sV.n)
        te.matmul(ps_x[:, 0:64], c_ones64[:, :], cdd[:, :], start=True, stop=True)
        sPE.inc(te.nop())
        ve.wait_ge(sPE.h, sPE.n)
        sV.inc(ve.tensor_copy(dbc[:, :], ps_x[:, 0:64]))
        qe.wait_ge(sV.h, sV.n)
        sQs.inc(qe.dma_start(out=cs_hbm.ap(), in_=csc[:, :]), 16)
        ST["cs_w"] = sQs.n

    def emit_xdt(c):
        for stile in range(2):
            for wv in range(16):
                te.wait_ge(sV.h, sV.n)
                for jj in range(2):
                    jch = wv * 2 + jj
                    te.transpose(pt[jj][:, 0:128],
                                 x_ssm[:, jch * 1024 + c * CHUNK + stile * 128:
                                       jch * 1024 + c * CHUNK + (stile + 1) * 128],
                                 c_ident[:, :])
                sPE.inc(te.nop())
                ve.wait_ge(sPE.h, sPE.n)
                for jj in range(2):
                    jch = wv * 2 + jj
                    dsl = dtT.ap()[:, stile * 64 + 2 * jch: stile * 64 + 2 * jch + 2]
                    in1 = bass.AP(dsl.tensor, dsl.offset,
                                  [list(dsl.ap[0]), list(dsl.ap[1]), [0, 64]])
                    sV.inc(ve.scalar_tensor_tensor(
                        Xdt[:, stile * D_INNER + jch * 128:
                            stile * D_INNER + (jch + 1) * 128]
                        .rearrange("p (h e) -> p h e", h=2),
                        pt[jj][:, 0:128]
                        .rearrange("p (h e) -> p h e", h=2),
                        1.0, in1, ALU.bypass, ALU.mult))
                te.wait_ge(sV.h, sV.n)
            te.transpose(pt[0][:, 0:128],
                         Bfm[:, c * CHUNK + stile * 128:c * CHUNK + (stile + 1) * 128],
                         c_ident[:, :])
            sPE.inc(te.nop())
            ve.wait_ge(sPE.h, sPE.n)
            sV.inc(ve.tensor_copy(Btm[:, stile * 128:(stile + 1) * 128],
                                  pt[0][:, 0:128]))
            te.wait_ge(sV.h, sV.n)

    def emit_states(c):
        ve.wait_ge(sPE.h, sPE.n)
        for stile in range(2):
            dsl = decT.ap()[:, stile * 64:stile * 64 + 64]
            in1 = bass.AP(dsl.tensor, dsl.offset,
                          [list(dsl.ap[0]), list(dsl.ap[1]), [0, 64]])
            sV.inc(ve.scalar_tensor_tensor(
                Xdt[:, stile * D_INNER:(stile + 1) * D_INNER]
                .rearrange("p (h e) -> p h e", h=64),
                Xdt[:, stile * D_INNER:(stile + 1) * D_INNER]
                .rearrange("p (h e) -> p h e", h=64),
                1.0, in1, ALU.bypass, ALU.mult))
        upd = {}
        for blk in range(8):
            pb = ps[0]
            te.wait_ge(sV.h, sV.n)
            if blk >= 1:
                te.wait_ge(sV.h, upd[blk - 1])
            for stile in range(2):
                te.matmul(pb[:, :], Btm[:, stile * 128:(stile + 1) * 128],
                          Xdt[:, stile * D_INNER + blk * 512:
                              stile * D_INNER + (blk + 1) * 512],
                          start=(stile == 0), stop=(stile == 1))
            sPE.inc(te.nop())
            ve.wait_ge(sPE.h, sPE.n)
            dsl = dbc.ap()[:, blk * 8:blk * 8 + 8]
            in1 = bass.AP(dsl.tensor, dsl.offset,
                          [list(dsl.ap[0]), list(dsl.ap[1]), [0, 64]])
            sV.inc(ve.scalar_tensor_tensor(
                state[:, blk * 512:(blk + 1) * 512].rearrange("p (h e) -> p h e", h=8),
                state[:, blk * 512:(blk + 1) * 512].rearrange("p (h e) -> p h e", h=8),
                1.0, in1, ALU.bypass, ALU.mult))
            sV.inc(ve.tensor_add(state[:, blk * 512:(blk + 1) * 512],
                                 state[:, blk * 512:(blk + 1) * 512], pb[:, :]))
            upd[blk] = sV.n

    def emit_scan_main(c):
        if c > 0:
            ge.wait_ge(sV.h, ST["yg_last"])
        sQg.inc(ge.dma_start(
            out=szbuf[:, :],
            in_=z_hbm.ap()[:, :, c * CHUNK:(c + 1) * CHUNK].transpose([1, 0, 2])), 16)
        se.wait_ge(sQg.h, sQg.n)
        sS.inc(se.activation(szbuf[:, :], szbuf[:, :], AF.Silu))
        sz_done = sS.n

        emit_dt_cs(c)
        emit_xdt(c)

        te.wait_ge(sV.h, sV.n)
        gps = (ps_x, ps[4])
        for stile in range(2):
            te.matmul(gps[stile][:, 0:256],
                      Bfm[:, c * CHUNK + stile * 128:c * CHUNK + (stile + 1) * 128],
                      Cfm[:, c * CHUNK:(c + 1) * CHUNK],
                      start=True, stop=True)
        sPE.inc(te.nop())
        ve.wait_ge(sPE.h, sPE.n)
        for stile in range(2):
            sV.inc(ve.tensor_mul(Gm[:, stile * 256:(stile + 1) * 256],
                                 gps[stile][:, 0:256],
                                 c_mask[:, stile * 256:(stile + 1) * 256]))
        gm_done = sV.n

        ypair = 0
        y_evac = {}
        cexp_free = {}
        for g in range(8):
            if g >= 1:
                qe.wait_ge(sV.h, ST["t_last"])
                qe.wait_ge(sS.h, ST["exprb_s"])
            qe.wait_ge(sQs.h, ST["cs_w"])
            src = bass.AP(cs_hbm, g * 8 * CHUNK, [[0, 128], [1, 2048]])
            sQs.inc(qe.dma_start(out=RBb[:, :], in_=src), 16)
            rb_n = sQs.n
            se.wait_ge(sQs.h, rb_n)
            if g >= 1:
                se.wait_ge(sV.h, ST["cexp_v"])
            sS.inc(se.activation(ExpRB[:, :], RBb[:, :], AF.Exp))
            ST["exprb_s"] = sS.n
            ve.wait_ge(sS.h, sS.n)
            if g >= 1:
                ve.wait_ge(sPE.h, cexp_free[g - 1])
            cslice = Cfm.ap()[:, c * CHUNK:(c + 1) * CHUNK]
            sV.inc(ve.tensor_mul(
                Cexp[0][:, :].rearrange("p (h c2) -> p h c2", h=8),
                bass.AP(cslice.tensor, cslice.offset,
                        [list(cslice.ap[0]), [0, 8], list(cslice.ap[1])]),
                ExpRB[:, :].rearrange("p (h c2) -> p h c2", h=8)))
            ST["cexp_v"] = sV.n
            for stile in range(2):
                ve.wait_ge(sQs.h, rb_n)
                ve.wait_ge(sS.h, ST.get(f"e0_{stile}", 0))
                for hh in range(8):
                    sV.inc(ve.tensor_scalar(
                        Tbuf[:, hh * 256:(hh + 1) * 256],
                        RBb[:, hh * 256:(hh + 1) * 256],
                        cstT[:, stile * 64 + g * 8 + hh: stile * 64 + g * 8 + hh + 1],
                        0.0, ALU.subtract, ALU.min))
                tpass = sV.n
                se.wait_ge(sV.h, tpass)
                if g >= 1:
                    se.wait_ge(sPE.h, ST["ymm_g"])
                sS.inc(se.activation(Ebuf[:, stile * 2048:(stile + 1) * 2048],
                                     Tbuf[:, :], AF.Exp))
                ST[f"e0_{stile}"] = sS.n
                ve.wait_ge(sS.h, sS.n)
                ve.wait_ge(sV.h, gm_done)
                gsl = Gm.ap()[:, stile * 256:(stile + 1) * 256]
                sV.inc(ve.tensor_mul(
                    Ebuf[:, stile * 2048:(stile + 1) * 2048]
                    .rearrange("p (h c2) -> p h c2", h=8),
                    Ebuf[:, stile * 2048:(stile + 1) * 2048]
                    .rearrange("p (h c2) -> p h c2", h=8),
                    bass.AP(gsl.tensor, gsl.offset,
                            [list(gsl.ap[0]), [0, 8], list(gsl.ap[1])])))
                ST[f"e_{stile}"] = sV.n
            ST["t_last"] = sV.n
            te.wait_ge(sV.h, ST["e_1"])
            for pr in range(4):
                hp = g * 4 + pr
                h0 = g * 8 + pr * 2
                if ypair >= 2:
                    ve.wait_ge(sPE.h, ST.get(f"ymm_p{ypair-2}", 0))
                sV.inc(ve.tensor_scalar_mul(ddtile[ypair % 2][:, :], c_ident[:, :],
                                            c_drep[:, hp:hp + 1]))
                sV.inc(ve.tensor_copy(stcast[ypair % 2][:, :],
                                      state[:, h0 * 64:(h0 + 2) * 64]))
                dd_v = sV.n
                pA = ps[1 + 2 * (ypair % 2)]
                pB = ps[2 + 2 * (ypair % 2)]
                if ypair >= 2:
                    te.wait_ge(sV.h, y_evac[ypair - 2])
                te.wait_ge(sV.h, dd_v)
                pYs = (pA[0:64, 0:256], pB[64:128, 0:256])
                for colg in range(2):
                    for stile in range(2):
                        h = h0 + colg
                        te.matmul(pYs[colg],
                                  Xdt[:, stile * D_INNER + h * 64:
                                      stile * D_INNER + (h + 1) * 64],
                                  Ebuf[:, stile * 2048 + (pr * 2 + colg) * 256:
                                       stile * 2048 + (pr * 2 + colg + 1) * 256],
                                  start=(stile == 0), stop=False,
                                  tile_position=(0, colg * 64))
                xs = x_ssm[:, hp * 1024 + c * CHUNK:hp * 1024 + (c + 1) * CHUNK]
                te.matmul(pYs[0], ddtile[ypair % 2][:, 0:64], xs,
                          start=False, stop=False, tile_position=(0, 0))
                te.matmul(pYs[1], ddtile[ypair % 2][:, 64:128], xs,
                          start=False, stop=False, tile_position=(0, 64))
                for colg in range(2):
                    te.matmul(pYs[colg],
                              stcast[ypair % 2][:, colg * 64:(colg + 1) * 64],
                              Cexp[0][:, (pr * 2 + colg) * 256:
                                          (pr * 2 + colg + 1) * 256],
                              start=False, stop=True,
                              tile_position=(0, colg * 64))
                ST[f"ymm_p{ypair}"] = sPE.inc(te.nop())
                ve.wait_ge(sPE.h, sPE.n)
                ve.wait_ge(sS.h, sz_done)
                sV.inc(ve.tensor_mul(
                    x_ssm[0:64, hp * 1024 + c * CHUNK:hp * 1024 + (c + 1) * CHUNK],
                    pYs[0], szbuf[0:64, hp * 256:(hp + 1) * 256]))
                sV.inc(ve.tensor_mul(
                    x_ssm[64:128, hp * 1024 + c * CHUNK:hp * 1024 + (c + 1) * CHUNK],
                    pYs[1], szbuf[64:128, hp * 256:(hp + 1) * 256]))
                y_evac[ypair] = sV.n
                ypair += 1
            ST["ymm_g"] = sPE.n
            cexp_free[g] = sPE.n
        ST["yg_last"] = sV.n

        for jj in range(NFI):
            ve.wait_ge(sV.h, y_evac[jj])
            if jj > 0:
                ve.wait_ge(sPE.h, sPE.n)
            sV.inc(ve.tensor_mul(
                ySq[:, :],
                x_ssm[:, jj * 1024 + c * CHUNK:jj * 1024 + (c + 1) * CHUNK],
                x_ssm[:, jj * 1024 + c * CHUNK:jj * 1024 + (c + 1) * CHUNK]))
            te.wait_ge(sV.h, sV.n)
            te.matmul(ps_x[0:1, 0:256], c_onescb[:, :], ySq[:, :],
                      start=(jj == 0), stop=(jj == NFI - 1))
            sPE.inc(te.nop())
        se.wait_ge(sPE.h, sPE.n)
        sS.inc(se.activation(rsq_r[0:1, :], ps_x[0:1, 0:256], AF.Sqrt,
                             scale=1.0 / D_INNER, bias=c_eps[0:1, 0:1]))
        ve.wait_ge(sS.h, sS.n)
        sV.inc(ve.reciprocal(rsq_r[0:1, :], rsq_r[0:1, :]))
        te.wait_ge(sV.h, sV.n)
        te.matmul(ps[0][:, 0:256], c_onesf[:, :], rsq_r[0:1, :],
                  start=True, stop=True)
        sPE.inc(te.nop())
        ve.wait_ge(sPE.h, sPE.n)
        for jj in range(NFI):
            sV.inc(ve.scalar_tensor_tensor(
                x_ssm[:, jj * 1024 + c * CHUNK:jj * 1024 + (c + 1) * CHUNK],
                x_ssm[:, jj * 1024 + c * CHUNK:jj * 1024 + (c + 1) * CHUNK],
                c_nw[:, jj:jj + 1], ps[0][:, 0:256], ALU.mult, ALU.mult))
        ST["yn_done"] = sV.n

        emit_states(c)

    emit_ln(x_carry, zeros3, spill_xn=False)
    bar()
    if ckpt("LNc"):
        ctx.close()
        return nc
    emit_inproj(list(range(32, 67)), spill_z=False)
    bar()
    ve.memset(state[:, :], 0.0)
    sV.inc(ve.nop())
    for c in range(NCHUNK):
        emit_dt_cs(c)
        emit_xdt(c)
        emit_states(c)
    ve.wait_ge(sV.h, sV.n)
    sV.inc(ve.tensor_scalar_mul(state[:, :], state[:, :], c_gate[:, :]))
    bar()
    if ckpt("SCc"):
        ctx.close()
        return nc
    if tap is not None and tap[0] == "carry_state":
        sQs.inc(qe.dma_start(out=tap_t.ap(), in_=state[:, :]), 16)
        qe.wait_ge(sQs.h, sQs.n)
        bar()

    emit_ln(x_rows, x_prev3, spill_xn=True)
    bar()
    if ckpt("LN"):
        ctx.close()
        return nc
    tap_now("xnc0", xn_fm[:, 3:1027])
    tap_now("xn", xn_fm[:, 0:PITCH * 2])
    emit_inproj(list(range(0, 67)), spill_z=True)
    bar()
    if ckpt("IP"):
        ctx.close()
        return nc
    tap_now("xssm", x_ssm[:, 0:4096])
    tap_now("bfm", Bfm[:, :])
    tap_now("cfm", Cfm[:, :])
    tap_now("dtraw", dt_raw[:, :])
    for c in range(NCHUNK):
        emit_scan_main(c)
        if tap is not None and tap[0] == f"state{c}":
            ve.wait_ge(sV.h, sV.n)
            sV.inc(ve.nop())
            qe.wait_ge(sV.h, sV.n)
            sQs.inc(qe.dma_start(out=tap_t.ap(), in_=state[:, :]), 16)
            qe.wait_ge(sQs.h, sQs.n)
    bar()
    if ckpt("SC"):
        ctx.close()
        return nc
    tap_now("yn", x_ssm[:, 0:4096])

    nw_i = 0
    op_done = {}
    m4_evac = 0
    for nb in range(4):
        for wave in range(2):
            for k in range(NFI):
                par = nw_i % 2
                if nw_i >= 2:
                    qe.wait_ge(sG.h, op_done[nw_i - 2][0])
                    qe.wait_ge(sS.h, op_done[nw_i - 2][1])
                sQs.inc(qe.dma_start(out=wstg[par][:, 0:512],
                                     in_=w_out.ap()[k][:, nb * 512:(nb + 1) * 512]), 16)
                slab_n = sQs.n
                if nw_i >= 2:
                    ge.wait_ge(sPE.h, op_done[nw_i - 2][2])
                    se.wait_ge(sPE.h, op_done[nw_i - 2][2])
                ge.wait_ge(sQs.h, slab_n)
                sG.inc(ge.tensor_copy(wbf[par][:, 0:256], wstg[par][:, 0:256]))
                se.wait_ge(sQs.h, slab_n)
                sS.inc(se.activation(wbf[par][:, 256:512], wstg[par][:, 256:512],
                                     AF.Copy))
                sG.inc(ge.tensor_copy(convd[0][:, :], wstg[par][:, 0:512]))
                sS.inc(se.activation(convd[1][:, :], wstg[par][:, 0:512],
                                     AF.Copy))
                te.wait_ge(sG.h, sG.n)
                te.wait_ge(sS.h, sS.n)
                if k == 0:
                    te.wait_ge(sV.h, m4_evac)
                for tb in range(4):
                    tbg = wave * 4 + tb
                    te.matmul(ps[tb][:, :],
                              x_ssm[:, k * 1024 + tbg * 128:k * 1024 + (tbg + 1) * 128],
                              wbf[par][:, 0:512],
                              start=(k == 0), stop=(k == NFI - 1))
                op_done[nw_i] = (sG.n, sS.n, sPE.inc(te.nop()))
                nw_i += 1
            ve.wait_ge(sPE.h, sPE.n)
            for tb in range(4):
                tbg = wave * 4 + tb
                ve.wait_ge(sQg.h, sQg.n)
                sV.inc(ve.tensor_copy(s_stage[tb % 2][:, :], ps[tb][:, :]))
                ge.wait_ge(sV.h, sV.n)
                sQg.inc(ge.dma_start(out=s_hbm.ap()[tbg][:, nb * 512:(nb + 1) * 512],
                                     in_=s_stage[tb % 2][:, :]), 16)
            m4_evac = sV.n
    bar()
    if ckpt("OP"):
        ctx.close()
        return nc
    if tap is not None and tap[0] == "s":
        sQs.inc(qe.dma_start(out=tap_t.ap(), in_=s_hbm.ap()), 16)
        qe.wait_ge(sQs.h, sQs.n)
        bar()

    hdone = {}
    h12_done = {}
    for a in range(NDM):
        if a >= 2:
            ve.wait_ge(sPE.h, h12_done[a - 2])
        sV.inc(ve.tensor_scalar_mul(hbtile[a % 2][:, :], c_h128[:, :],
                                    c_suug[:, a:a + 1]))
        hp_v = sV.n
        if a >= 2:
            qe.wait_ge(sV.h, hdone[a - 2])
        sQs.inc(qe.dma_start(out=xnld[a % 2][:, :], in_=xn_hbm.ap()[a]), 16)
        te.wait_ge(sQs.h, sQs.n)
        te.wait_ge(sV.h, hp_v)
        if a >= 2:
            te.wait_ge(sQg.h, ST.get(f"perm1_{a-2}", 0))
            te.wait_ge(sV.h, hdone[a - 2])
        pa, pb = ps[2 * (a % 2)], ps[2 * (a % 2) + 1]
        te.matmul(pa[:, :], hbtile[a % 2][:, :], xnld[a % 2][:, 0:512],
                  start=True, stop=True)
        te.matmul(pb[:, :], hbtile[a % 2][:, :], xnld[a % 2][:, 512:1024],
                  start=True, stop=True)
        h12_done[a] = sPE.inc(te.nop())
        ve.wait_ge(sPE.h, sPE.n)
        sV.inc(ve.tensor_copy(v1a[a % 2][:, 0:512], pa[:, :]))
        sV.inc(ve.tensor_copy(v1a[a % 2][:, 512:1024], pb[:, :]))
        hdone[a] = sV.n
        ge.wait_ge(sV.h, sV.n)
        for i8 in range(8):
            sQg.inc(ge.dma_start(
                out=h1perm[16 * i8 + a:16 * i8 + a + 1, :]
                .rearrange("p (j t2) -> p j t2", j=16),
                in_=v1a[a % 2][16 * i8:16 * (i8 + 1), :]), 16)
        ST[f"perm1_{a}"] = sQg.n
    ge.wait_ge(sQg.h, sQg.n)
    bar()
    tap_now("h1pre", h1perm[:, 0:2048])
    for jv in range(16):
        pa = ps[jv % 4]
        if jv >= 4:
            te.wait_ge(sV.h, hdone[100 + jv - 4])
        for th in range(2):
            if th == 1:
                te.wait_ge(sV.h, sV.n)
            te.matmul(pa[:, :], c_h16[:, :],
                      h1perm[:, jv * 1024 + th * 512:jv * 1024 + (th + 1) * 512],
                      start=True, stop=True)
            sPE.inc(te.nop())
            ve.wait_ge(sPE.h, sPE.n)
            sV.inc(ve.tensor_copy(
                h1perm[:, jv * 1024 + th * 512:jv * 1024 + (th + 1) * 512],
                pa[:, :]))
        hdone[100 + jv] = sV.n
    bar()
    if ckpt("F1"):
        ctx.close()
        return nc
    tap_now("h1", h1perm[:, 0:2048])

    for a in range(NA):
        if a >= 2:
            ve.wait_ge(sPE.h, ST.get(f"f2hb_{a-2}_1", 0))
        sV.inc(ve.tensor_scalar_mul(hbtile[a % 2][:, :], c_h128[:, :],
                                    hcol[:, a:a + 1]))
        hb_v = sV.n
        if a >= 1:
            qe.wait_ge(sG.h, ST[f"f2cast_{a-1}"][0])
        sQs.inc(qe.dma_start(out=wstg[0][:, :],
                             in_=w_up.ap()[a].rearrange("p k m -> p (k m)")), 16)
        up_n = sQs.n
        sQa.inc(se.dma_start(out=wstg[1][:, :],
                             in_=w_gate.ap()[a].rearrange("p k m -> p (k m)")), 16)
        gt_n = sQa.n
        if a >= 1:
            ge.wait_ge(sPE.h, ST[f"f2pe_{a-1}"])
            se.wait_ge(sPE.h, ST[f"f2pe_{a-1}"])
        ge.wait_ge(sQs.h, up_n)
        sG.inc(ge.tensor_copy(wbf[0][:, :], wstg[0][:, :]))
        se.wait_ge(sQa.h, gt_n)
        sS.inc(se.activation(wbf[1][:, :], wstg[1][:, :], AF.Copy))
        ST[f"f2cast_{a}"] = (sG.n, sS.n)
        te.wait_ge(sG.h, sG.n)
        te.wait_ge(sS.h, sS.n)
        te.wait_ge(sV.h, hb_v)
        for th in range(2):
            pu, pg = ps[2 * (a % 2)], ps[2 * (a % 2) + 1]
            if a >= 2:
                te.wait_ge(sV.h, ST.get(f"f2v_{a-2}_{th}", 0))
            for k in range(NDM):
                te.matmul(pu[:, :], wbf[0][:, k * 128:(k + 1) * 128],
                          h1perm[:, k * 1024 + th * 512:k * 1024 + (th + 1) * 512],
                          start=(k == 0), stop=(k == NDM - 1))
                te.matmul(pg[:, :], wbf[1][:, k * 128:(k + 1) * 128],
                          h1perm[:, k * 1024 + th * 512:k * 1024 + (th + 1) * 512],
                          start=(k == 0), stop=(k == NDM - 1))
            mm_n = sPE.inc(te.nop())
            se.wait_ge(sPE.h, mm_n)
            if a >= 1:
                se.wait_ge(sV.h, ST.get(f"f2v_{a-1}_{th}", 0))
            sS.inc(se.activation(sgbuf[th][:, :], pg[:, :], AF.Silu,
                                 scale=c_wscg[:, a:a + 1]))
            ve.wait_ge(sS.h, sS.n)
            if a >= 1:
                ve.wait_ge(sPE.h, ST.get(f"f2hb_{a-1}_{th}", 0))
            sV.inc(ve.tensor_mul(vbuf[th][:, :], pu[:, :], sgbuf[th][:, :]))
            ST[f"f2v_{a}_{th}"] = sV.n
            te.wait_ge(sV.h, sV.n)
            if a >= 1:
                te.wait_ge(sV.h, ST.get(f"f2e_{a-1}_{th}", 0))
            hbps = ps[4] if th == 0 else ps_x
            te.matmul(hbps[:, :], hbtile[a % 2][:, :],
                      vbuf[th][:, :], start=True, stop=True)
            ST[f"f2hb_{a}_{th}"] = sPE.inc(te.nop())
            ve.wait_ge(sPE.h, sPE.n)
            if a >= 1:
                ve.wait_ge(sQg.h, ST.get(f"perm2_{a-1}_{th}", 0))
            sV.inc(ve.tensor_copy(vbuf[th][:, :], hbps[:, :]))
            ST[f"f2e_{a}_{th}"] = sV.n
            ge.wait_ge(sV.h, sV.n)
            for p2 in range(2):
                sQg.inc(ge.dma_start(
                    out=bass.AP(H2buf.ap().tensor,
                                H2buf.ap().offset + (64 * p2 + a) * 65536 + th * 512,
                                [[65536, 1], [1024, 64], [1, 512]]),
                    in_=vbuf[th][64 * p2:64 * (p2 + 1), :]), 16)
            ST[f"perm2_{a}_{th}"] = sQg.n
        ST[f"f2pe_{a}"] = sPE.n
    ge.wait_ge(sQg.h, sQg.n)
    bar()
    if ckpt("F2"):
        ctx.close()
        return nc

    h64_ev = {}
    for gch in range(NA):
        for tblk in range(2):
            idx = gch * 2 + tblk
            pa = ps[4] if idx % 2 == 0 else ps_x
            if idx >= 2:
                te.wait_ge(sV.h, h64_ev[idx - 2])
            te.matmul(pa[:, :], c_h64[:, :],
                      H2buf[:, gch * 1024 + tblk * 512:gch * 1024 + (tblk + 1) * 512],
                      start=True, stop=True)
            sPE.inc(te.nop())
            ve.wait_ge(sPE.h, sPE.n)
            sV.inc(ve.tensor_copy(
                H2buf[:, gch * 1024 + tblk * 512:gch * 1024 + (tblk + 1) * 512],
                pa[:, :]))
            h64_ev[idx] = sV.n
    bar()
    if ckpt("F3"):
        ctx.close()
        return nc
    tap_now("h2", H2buf[:, 0:2048])

    for i4 in range(4):
        if i4 > 0:
            qe.wait_ge(sPE.h, sPE.n)
        sQs.inc(qe.dma_start(out=wdp_row[0:1, :], in_=wsc_dp.ap()[i4:i4 + 1, :]), 16)
        te.wait_ge(sQs.h, sQs.n)
        if i4 > 0:
            te.wait_ge(sV.h, sV.n)
        te.matmul(ps_x[:, 0:512], c_onesf[:, :], wdp_row[0:1, :],
                  start=True, stop=True)
        sPE.inc(te.nop())
        ve.wait_ge(sPE.h, sPE.n)
        sV.inc(ve.tensor_scalar(wscRB[:, i4 * 512:(i4 + 1) * 512],
                                ps_x[:, 0:512], SC8192, None, ALU.mult))
    bar()

    nw_i = 0
    f4_done = {}
    f4_evac = 0
    piece_hist = []
    for nb in range(4):
        for th in range(2):
            for gch in range(NA):
                par = nw_i % 2
                if nw_i >= 2:
                    qe.wait_ge(sG.h, f4_done[nw_i - 2][0])
                    qe.wait_ge(sS.h, f4_done[nw_i - 2][1])
                sQs.inc(qe.dma_start(
                    out=wstg[par][:, 0:512],
                    in_=w_down.ap()[gch][:, nb * 512:(nb + 1) * 512]), 16)
                slab_n = sQs.n
                if nw_i >= 2:
                    ge.wait_ge(sPE.h, f4_done[nw_i - 2][2])
                    se.wait_ge(sPE.h, f4_done[nw_i - 2][2])
                ge.wait_ge(sQs.h, slab_n)
                sG.inc(ge.tensor_copy(wbf[par][:, 0:256], wstg[par][:, 0:256]))
                se.wait_ge(sQs.h, slab_n)
                sS.inc(se.activation(wbf[par][:, 256:512], wstg[par][:, 256:512],
                                     AF.Copy))
                sG.inc(ge.tensor_copy(convd[0][:, :], wstg[par][:, 0:512]))
                sS.inc(se.activation(convd[1][:, :], wstg[par][:, 0:512],
                                     AF.Copy))
                te.wait_ge(sG.h, sG.n)
                te.wait_ge(sS.h, sS.n)
                if gch == 0:
                    te.wait_ge(sV.h, f4_evac)
                for tb in range(4):
                    te.matmul(ps[tb][:, :],
                              H2buf[:, gch * 1024 + th * 512 + tb * 128:
                                    gch * 1024 + th * 512 + (tb + 1) * 128],
                              wbf[par][:, 0:512],
                              start=(gch == 0), stop=(gch == NA - 1))
                f4_done[nw_i] = (sG.n, sS.n, sPE.inc(te.nop()))
                nw_i += 1
            for tb in range(4):
                tbg = th * 4 + tb
                pi = tb % 2
                prev = piece_hist[-2] if len(piece_hist) >= 2 else 0
                se.wait_ge(sV.h, prev)
                sQa.inc(se.dma_start(
                    out=sp_b[pi][:, :],
                    in_=s_hbm.ap()[tbg][:, nb * 512:(nb + 1) * 512]), 16)
                qe.wait_ge(sV.h, prev)
                sQs.inc(qe.dma_start(
                    out=xpf[pi][:, :],
                    in_=x_rows.ap()[tbg * 128:(tbg + 1) * 128,
                                    nb * 512:(nb + 1) * 512]), 16)
                ve.wait_ge(sPE.h, sPE.n)
                ve.wait_ge(sQa.h, sQa.n)
                ve.wait_ge(sQs.h, sQs.n)
                ve.wait_ge(sQg.h, sQg.n)
                sV.inc(ve.tensor_mul(gp_f[:, :], ps[tb][:, :],
                                     wscRB[:, nb * 512:(nb + 1) * 512]))
                se.wait_ge(sV.h, sV.n)
                sS.inc(se.activation(gp_b[pi][:, :], gp_f[:, :], AF.Silu))
                ve.wait_ge(sS.h, sS.n)
                sV.inc(ve.tensor_mul(gp_b[pi][:, :], gp_b[pi][:, :], sp_b[pi][:, :]))
                sV.inc(ve.tensor_add(opf[pi][:, :], xpf[pi][:, :], gp_b[pi][:, :]))
                piece_hist.append(sV.n)
                ge.wait_ge(sV.h, sV.n)
                sQg.inc(ge.dma_start(
                    out=out_t.ap()[tbg * 128:(tbg + 1) * 128,
                                   nb * 512:(nb + 1) * 512],
                    in_=opf[pi][:, :]), 16)
            f4_evac = sV.n
    ge.wait_ge(sQg.h, sQg.n)
    qe.wait_ge(sQg.h, sQg.n)

    ctx.close()
    return nc


def kernel(**inputs):
    shared = _stage_shared(inputs)
    in_maps = []
    for core in range(8):
        m = dict(shared)
        m.update(_stage_core(inputs, core))
        in_maps.append(m)
    nc = build_nc()
    res = run_bass_kernel_spmd(nc, in_maps, core_ids=list(range(8)))
    out = np.empty((4, 2048, D_MODEL), np.float32)
    for core in range(8):
        b_, half = core // 2, core % 2
        out[b_, half * T:(half + 1) * T, :] = res.results[core]["out"]
    return out


# version 03
# speedup vs baseline: 1.0150x; 1.0150x over previous
import math
from contextlib import ExitStack

import numpy as np
import ml_dtypes

import concourse.bass as bass
import concourse.mybir as mybir
from concourse.bass_utils import run_bass_kernel_spmd

F32 = mybir.dt.float32
BF16 = mybir.dt.bfloat16
AF = mybir.ActivationFunctionType
ALU = mybir.AluOpType
AX = mybir.AxisListType
BF = ml_dtypes.bfloat16

D_MODEL = 2048
D_INNER = 4096
HEADDIM = 64
INTER = 8192
CHUNK = 256
NCHUNK = 4
T = 1024
EPS = 1e-5
SC2048 = 1.0 / math.sqrt(2048.0)
SC8192 = 1.0 / math.sqrt(8192.0)

NZ = 32
NXBC = 34
NDIP = 67
NDM = 16
NFI = 32
NA = 64
PITCH = 1056
KB = 1024


def _had(n):
    h = np.array([[1.0]], dtype=np.float32)
    while h.shape[0] < n:
        h = np.block([[h, h], [h, -h]])
    return h


def _stage_shared(inputs):
    d = {}
    w = np.asarray(inputs["L0_attn_in_proj_w"])
    wp = np.zeros((NDIP * 128, D_MODEL), np.float32)
    wp[:8512] = w
    wt = np.ascontiguousarray(wp.T)
    d["w_in"] = np.ascontiguousarray(
        wt.reshape(NDM, 128, NDIP, 128).transpose(2, 1, 0, 3))

    q = np.arange(128)
    j = np.arange(16)
    fmap = (q[None, :] % 16) * 128 + 16 * (q[None, :] // 16) + j[:, None]
    wu = np.asarray(inputs["L0_ffn_W_up"]).reshape(NA, 128, D_MODEL)
    wg = np.asarray(inputs["L0_ffn_W_gate"]).reshape(NA, 128, D_MODEL)
    d["w_up"] = np.ascontiguousarray(wu[:, :, fmap].transpose(0, 3, 2, 1))
    d["w_gate"] = np.ascontiguousarray(wg[:, :, fmap].transpose(0, 3, 2, 1))

    wd = np.asarray(inputs["L0_ffn_W_down"])
    g = np.arange(NA)
    p = np.arange(128)
    f2map = (p[None, :] % 64) * 128 + 64 * (p[None, :] // 64) + g[:, None]
    d["w_down"] = np.ascontiguousarray(wd[:, f2map].transpose(1, 2, 0))

    wo = np.asarray(inputs["L0_attn_out_proj_w"])
    d["w_out"] = np.ascontiguousarray(wo.T.reshape(NFI, 128, D_MODEL))

    d["ident"] = np.eye(128, dtype=BF)
    d["identf"] = np.eye(128, dtype=np.float32)
    d["h128"] = _had(128).astype(BF)
    d["h16blk"] = np.kron(np.eye(8), _had(16)).astype(BF)
    d["h64blk"] = np.kron(np.eye(2), _had(64)).astype(BF)
    s = np.arange(CHUNK)[:, None]
    c = np.arange(CHUNK)[None, :]
    m = (s <= c).astype(BF)
    d["mask01"] = np.ascontiguousarray(
        np.concatenate([m[0:128], m[128:256]], axis=1))
    d["ones_f"] = np.ones((1, 128), np.float32)
    d["ones_bf"] = np.ones((1, 128), BF)
    d["ones_cb"] = np.ones((128, 1), BF)

    f32 = np.float32
    d["dt_bias"] = np.asarray(inputs["L0_attn_dt_bias"]).astype(f32).reshape(64, 1)
    d["a_log"] = np.asarray(inputs["L0_attn_A_log"]).astype(f32).reshape(64, 1)
    d["d_rep"] = np.ascontiguousarray(
        np.repeat(np.asarray(inputs["L0_attn_D"]).astype(f32), HEADDIM)
        .reshape(NFI, 128).T)
    d["nw"] = np.ascontiguousarray(
        np.asarray(inputs["L0_attn_norm_weight"]).astype(f32).reshape(NFI, 128).T)
    d["conv_w"] = np.ascontiguousarray(
        np.asarray(inputs["L0_attn_conv_w"]).astype(f32)
        .reshape(NXBC, 128, 4).transpose(1, 0, 2).reshape(128, NXBC * 4))
    d["conv_b"] = np.ascontiguousarray(
        np.asarray(inputs["L0_attn_conv_b"]).astype(f32).reshape(NXBC, 128).T)
    d["su_ug"] = np.ascontiguousarray(
        np.asarray(inputs["L0_ffn_SU_ug"]).astype(f32).reshape(NDM, 128).T)
    d["su_dp"] = np.ascontiguousarray(
        np.asarray(inputs["L0_ffn_SU_dp"]).astype(f32).reshape(NA, 128).T)
    wsug = np.asarray(inputs["L0_ffn_Wscale_ug"]).astype(f32)
    d["wsc_up"] = np.ascontiguousarray(wsug[:INTER].reshape(NA, 128).T)
    d["wsc_gate"] = np.ascontiguousarray(wsug[INTER:].reshape(NA, 128).T)
    d["wsc_dp"] = np.asarray(inputs["L0_ffn_Wscale_dp"]).astype(f32).reshape(4, 512)
    return d


def _stage_core(inputs, core):
    b, half = core // 2, core % 2
    x = np.asarray(inputs["L0_attn_u"])
    d = {}
    d["x_rows"] = np.ascontiguousarray(x[b, half * T:(half + 1) * T, :])
    if half == 1:
        d["x_prev3"] = np.ascontiguousarray(x[b, T - 3:T, :])
        d["x_carry"] = np.ascontiguousarray(x[b, 0:T, :])
        d["gate"] = np.ones((128, 1), np.float32)
    else:
        d["x_prev3"] = np.zeros((3, D_MODEL), np.float32)
        d["x_carry"] = np.zeros((T, D_MODEL), np.float32)
        d["gate"] = np.zeros((128, 1), np.float32)
    d["zeros3"] = np.zeros((3, D_MODEL), np.float32)
    return d


class Sem:
    def __init__(self, nc, ctx, name):
        self.h = ctx.enter_context(nc.semaphore(name))
        self.n = 0

    def inc(self, instr, k=1):
        instr.then_inc(self.h, k)
        self.n += k
        return self.n


class _TeWrap:

    def __init__(self, eng):
        self._e = eng
        self._last = None

    def matmul(self, *a, **k):
        self._last = self._e.matmul(*a, **k)
        return self._last

    def transpose(self, *a, **k):
        self._last = self._e.transpose(*a, **k)
        return self._last

    def wait_ge(self, *a, **k):
        return self._e.wait_ge(*a, **k)

    def nop(self):
        assert self._last is not None
        last, self._last = self._last, None
        return last


def build_nc(tap=None, stop_after=None):
    nc = bass.Bass("TRN2", target_bir_lowering=False, detect_race_conditions=False)
    ctx = ExitStack()
    te, ve, se, ge, qe = _TeWrap(nc.tensor), nc.vector, nc.scalar, nc.gpsimd, nc.sync

    def din(name, shape, dtype=F32):
        return nc.dram_tensor(name, list(shape), dtype, kind="ExternalInput")

    x_rows = din("x_rows", [T, D_MODEL])
    x_prev3 = din("x_prev3", [3, D_MODEL])
    x_carry = din("x_carry", [T, D_MODEL])
    zeros3 = din("zeros3", [3, D_MODEL])
    gate_in = din("gate", [128, 1])
    w_in = din("w_in", [NDIP, 128, NDM, 128])
    w_up = din("w_up", [NA, 128, NDM, 128])
    w_gate = din("w_gate", [NA, 128, NDM, 128])
    w_down = din("w_down", [NA, 128, D_MODEL])
    w_out = din("w_out", [NFI, 128, D_MODEL])
    ident = din("ident", [128, 128], BF16)
    identf = din("identf", [128, 128], F32)
    h128 = din("h128", [128, 128], BF16)
    h16blk = din("h16blk", [128, 128], BF16)
    h64blk = din("h64blk", [128, 128], BF16)
    mask01 = din("mask01", [128, 512], BF16)
    ones_f = din("ones_f", [1, 128])
    ones_bf = din("ones_bf", [1, 128], BF16)
    ones_cb = din("ones_cb", [128, 1], BF16)
    dt_bias = din("dt_bias", [64, 1])
    a_log = din("a_log", [64, 1])
    d_rep = din("d_rep", [128, NFI])
    nw_in = din("nw", [128, NFI])
    conv_w = din("conv_w", [128, NXBC * 4])
    conv_b = din("conv_b", [128, NXBC])
    su_ug = din("su_ug", [128, NDM])
    su_dp = din("su_dp", [128, NA])
    wsc_up = din("wsc_up", [128, NA])
    wsc_gate = din("wsc_gate", [128, NA])
    wsc_dp = din("wsc_dp", [4, 512])

    out_t = nc.dram_tensor("out", [T, D_MODEL], F32, kind="ExternalOutput")
    tap_t = None
    if tap is not None:
        tap_t = nc.dram_tensor("tap", list(tap[1]),
                               tap[2] if len(tap) > 2 else F32,
                               kind="ExternalOutput")

    z_hbm = nc.dram_tensor("z_hbm", [NZ, 128, T], BF16)
    s_hbm = nc.dram_tensor("s_hbm", [8, 128, D_MODEL], BF16)
    cs_hbm = nc.dram_tensor("cs_hbm", [64, CHUNK], F32)
    xn_hbm = nc.dram_tensor("xn_hbm", [NDM, 128, T], BF16)

    ps = [ctx.enter_context(nc.psum_tensor(f"ps{i}", [128, 512], F32))
          for i in range(5)]
    pt = [ctx.enter_context(nc.psum_tensor(f"pt{i}", [128, 1024], BF16))
          for i in range(2)]
    ps_x = ctx.enter_context(nc.psum_tensor("ps_x", [128, 512], F32))

    sQs = Sem(nc, ctx, "sQs")
    sQa = Sem(nc, ctx, "sQa")
    sQg = Sem(nc, ctx, "sQg")
    sPE = Sem(nc, ctx, "sPE")
    sV = Sem(nc, ctx, "sV")
    sS = Sem(nc, ctx, "sS")
    sG = Sem(nc, ctx, "sG")

    LIMIT = 212832
    _arena = nc.alloc_sbuf_tensor("arena_reserved", [128, LIMIT], mybir.dt.uint8)
    ABASE = nc.lookup_mloc(_arena).addr
    off = [0]

    def alloc(name, shape, dtype, at=None):
        per = int(np.prod(shape[1:])) * mybir.dt.size(dtype)
        if at is None:
            a = off[0]
            off[0] = (a + per + 31) // 32 * 32
        else:
            a = at
        assert a + per <= LIMIT, f"SBUF overflow: {name} @{a}+{per}"
        return nc.alloc_sbuf_tensor_at(name, list(shape), dtype, offset=ABASE + a)

    c_ident = alloc("c_ident", [128, 128], BF16)
    c_identf = alloc("c_identf", [128, 128], F32)
    c_h128 = alloc("c_h128", [128, 128], BF16)
    c_h16 = alloc("c_h16", [128, 128], BF16)
    c_h64 = alloc("c_h64", [128, 128], BF16)
    c_mask = alloc("c_mask", [128, 512], BF16)
    c_onesf = alloc("c_onesf", [1, 128], F32)
    c_onesb = alloc("c_onesb", [1, 128], BF16)
    c_onescb = alloc("c_onescb", [128, 1], BF16)
    c_dtb = alloc("c_dtb", [64, 1], F32)
    c_alog = alloc("c_alog", [64, 1], F32)
    c_aneg = alloc("c_aneg", [64, 1], F32)
    c_gate = alloc("c_gate", [128, 1], F32)
    c_drep = alloc("c_drep", [128, NFI], F32)
    c_nw = alloc("c_nw", [128, NFI], F32)
    c_convw = alloc("c_convw", [128, NXBC * 4], F32)
    c_convb = alloc("c_convb", [128, NXBC], F32)
    c_suug = alloc("c_suug", [128, NDM], F32)
    c_sudp = alloc("c_sudp", [128, NA], F32)
    c_wscup = alloc("c_wscup", [128, NA], F32)
    c_wscg = alloc("c_wscg", [128, NA], F32)
    dtc = alloc("dtc", [64, CHUNK], F32)
    csc = alloc("csc", [64, CHUNK], F32)
    dec = alloc("dec", [64, CHUNK], F32)
    cstT = alloc("cstT", [128, 128], F32)
    dtT = alloc("dtT", [128, 128], F32)
    decT = alloc("decT", [128, 128], F32)
    dbc = alloc("dbc", [128, 64], F32)
    cd_row = alloc("cd_row", [1, 64], F32)
    rsq_r = alloc("rsq_r", [1, CHUNK], F32)
    hcol = alloc("hcol", [128, NA], F32)
    c_ones64 = alloc("c_ones64", [64, 128], BF16)
    cdcol = alloc("cdcol", [64, 1], F32)
    cdcolb = alloc("cdcolb", [64, 1], BF16)
    cdd = alloc("cdd", [64, 64], BF16)
    c_eps = alloc("c_eps", [128, 1], F32)
    cshi = alloc("cshi", [64, CHUNK], BF16)
    cslo = alloc("cslo", [64, CHUNK], BF16)
    dtb16 = alloc("dtb16", [64, CHUNK], BF16)
    decb16 = alloc("decb16", [64, CHUNK], BF16)
    c_one = alloc("c_one", [128, 1], F32)
    c_zero = alloc("c_zero", [128, 1], F32)
    wstg = [alloc(f"wstg{i}", [128, 2048], F32) for i in range(2)]
    wbf = [alloc(f"wbf{i}", [128, 2048], BF16) for i in range(2)]
    convd = [alloc(f"convd{i}", [128, 512], BF16) for i in range(2)]
    ddtile = [alloc(f"ddtile{i}", [128, 128], BF16) for i in range(2)]
    stcast = [alloc(f"stcast{i}", [128, 128], BF16) for i in range(2)]
    hbtile = [alloc(f"hbtile{i}", [128, 128], BF16) for i in range(2)]
    sgbuf = [alloc(f"sgbuf{i}", [128, 512], BF16) for i in range(2)]
    vbuf = [alloc(f"vbuf{i}", [128, 512], BF16) for i in range(2)]
    PERS = off[0]
    assert PERS <= 46 * KB, PERS

    R1 = 46 * KB
    R2 = 80896
    R3 = 144 * KB
    xn_fm = alloc("xn_fm", [128, NDM * PITCH], BF16, at=R1)
    RBb = alloc("RBb", [128, 2048], F32, at=R1)
    Ebuf = alloc("Ebuf", [128, 2 * 2048], BF16, at=R1 + 8 * KB)
    Xdt = alloc("Xdt", [128, 2 * D_INNER], BF16, at=R1 + 16 * KB)
    s_stage = [alloc(f"s_stage{i}", [128, 512], BF16, at=R1 + i * KB)
               for i in range(2)]
    h1perm = alloc("h1perm", [128, 16 * 1024], BF16, at=R1)
    wscRB = alloc("wscRB", [128, D_MODEL], F32, at=R1)
    wdp_row = alloc("wdp_row", [1, 512], F32, at=R1 + 8 * KB)
    gp_f = alloc("gp_f", [128, 512], F32, at=R1 + 10 * KB)
    gp_b = [alloc(f"gp_b{i}", [128, 512], BF16, at=R1 + 12 * KB + i * KB)
            for i in range(2)]
    sp_b = [alloc(f"sp_b{i}", [128, 512], BF16, at=R1 + 14 * KB + i * KB)
            for i in range(2)]
    xpf = [alloc(f"xpf{i}", [128, 512], F32, at=R1 + 16 * KB + i * 2 * KB)
           for i in range(2)]
    opf = [alloc(f"opf{i}", [128, 512], F32, at=R1 + 20 * KB + i * 2 * KB)
           for i in range(2)]
    x_ssm = alloc("x_ssm", [128, NFI * 1024], BF16, at=R2)
    H2buf = alloc("H2buf", [128, NA * 1024], BF16, at=R2)
    state = alloc("state", [128, D_INNER], F32, at=R3 + 16 * KB)
    dt_raw = alloc("dt_raw", [64, T], F32, at=R3 + 56 * KB)
    xload = [alloc(f"xload{i}", [128, D_MODEL], F32, at=R3 + i * 8 * KB)
             for i in range(2)]
    xn_tm = [alloc(f"xn_tm{i}", [128, D_MODEL], BF16, at=R3 + 32 * KB + i * 4 * KB)
             for i in range(2)]
    sqd = alloc("sqd", [128, D_MODEL], F32, at=R3 + 40 * KB)
    xload.append(alloc("xload2", [128, D_MODEL], F32, at=R2))
    stm = [alloc(f"stm{i}", [128, 8], F32, at=R3 + 52 * KB + i * 64)
           for i in range(2)]
    stv = [alloc(f"stv{i}", [128, 8], F32, at=R3 + 52 * KB + 128 + i * 64)
           for i in range(2)]
    st2 = [alloc(f"st2{i}", [128, 8], F32, at=R3 + 52 * KB + 256 + i * 64)
           for i in range(2)]
    zstage = [alloc(f"zstage{i}", [128, T], BF16, at=R3 + i * 2 * KB)
              for i in range(2)]
    xbst = [alloc(f"xbst{i}", [128, PITCH], BF16, at=R3 + 4 * KB + i * 2176)
            for i in range(2)]
    Bfm = alloc("Bfm", [128, T], BF16, at=R3 + 10 * KB)
    Cfm = alloc("Cfm", [128, T], BF16, at=R3 + 12 * KB)
    Tbuf = alloc("Tbuf", [128, 2048], F32, at=R3)
    Gm = alloc("Gm", [128, 2 * CHUNK], BF16, at=R3 + 8 * KB)
    Btm = alloc("Btm", [128, 2 * 128], BF16, at=R3 + 9 * KB)
    ySq = alloc("ySq", [128, CHUNK], BF16, at=R3 + 9 * KB + 512)
    szbuf = alloc("szbuf", [128, NZ * CHUNK], BF16, at=R3 + 32 * KB)
    ExpRB = alloc("ExpRB", [128, 2048], BF16, at=R3 + 48 * KB)
    Cexp = [alloc("Cexp0", [128, 2048], BF16, at=R3 + 52 * KB)]
    xnld = [alloc(f"xnld{i}", [128, 1024], BF16, at=R3 + i * 2 * KB)
            for i in range(2)]
    v1a = [alloc(f"v1a{i}", [128, 1024], BF16, at=R3 + 4 * KB + i * 2 * KB)
           for i in range(2)]

    bar = nc.all_engine_barrier
    ST = {}
    _stopped = [False]

    def ckpt(name):
        if stop_after == name and not _stopped[0]:
            _stopped[0] = True
            qe.wait_ge(sV.h, sV.n)
            qe.wait_ge(sS.h, sS.n)
            qe.wait_ge(sPE.h, sPE.n)
            qe.wait_ge(sG.h, sG.n)
            qe.wait_ge(sQg.h, sQg.n)
            qe.wait_ge(sQa.h, sQa.n)
            sQs.inc(qe.dma_start(out=out_t.ap(), in_=x_rows.ap()), 16)
            qe.wait_ge(sQs.h, sQs.n)
        return _stopped[0]

    def tap_now(name, src_ap):
        if tap is not None and tap[0] == name:
            qe.wait_ge(sV.h, sV.n)
            qe.wait_ge(sS.h, sS.n)
            sQs.inc(qe.dma_start(out=tap_t.ap(), in_=src_ap), 16)
            qe.wait_ge(sQs.h, sQs.n)

    for dst, src in ((c_ident, ident), (c_identf, identf), (c_h128, h128),
                     (c_h16, h16blk), (c_h64, h64blk), (c_mask, mask01),
                     (c_onesf, ones_f), (c_onesb, ones_bf),
                     (c_onescb, ones_cb),
                     (c_dtb, dt_bias), (c_alog, a_log), (c_gate, gate_in),
                     (c_drep, d_rep), (c_nw, nw_in), (c_convw, conv_w),
                     (c_convb, conv_b), (c_suug, su_ug), (c_sudp, su_dp),
                     (c_wscup, wsc_up), (c_wscg, wsc_gate)):
        idx = tuple(slice(None) for _ in dst.shape)
        sQs.inc(qe.dma_start(out=dst[idx], in_=src.ap()), 16)
    ge.wait_ge(sQs.h, sQs.n)
    sG.inc(ge.memset(c_eps[:, :], EPS))
    sG.inc(ge.memset(c_one[:, :], 1.0))
    sG.inc(ge.memset(c_zero[:, :], 0.0))
    sG.inc(ge.memset(c_ones64[:, :], 1.0))
    se.wait_ge(sQs.h, sQs.n)
    sS.inc(se.activation(c_aneg[:, :], c_alog[:, :], AF.Exp))
    ve.wait_ge(sS.h, sS.n)
    sV.inc(ve.tensor_scalar_mul(c_aneg[:, :], c_aneg[:, :], -1.0))
    sV.inc(ve.tensor_scalar_mul(c_wscg[:, :], c_wscg[:, :], SC2048))
    sV.inc(ve.tensor_mul(hcol[:, :], c_sudp[:, :], c_wscup[:, :]))
    sV.inc(ve.tensor_scalar_mul(hcol[:, :], hcol[:, :], SC2048))
    bar()
    if ckpt("P0"):
        ctx.close()
        return nc

    def emit_ln(x_main, x_p3, spill_xn):
        apply_done = {}
        tr_done = {}
        dma_n = {}
        var_done = {}
        sqrt_gate = {}
        sqrt_done = {}

        def rw(r):
            return 3 if r == 0 else 128

        def ln_load(r):
            src = x_p3.ap() if r == 0 else x_main.ap()[(r - 1) * 128:r * 128, :]
            if r >= 3:
                qe.wait_ge(sV.h, apply_done[r - 3])
            sQs.inc(qe.dma_start(out=xload[r % 3][0:rw(r), :], in_=src), 16)
            dma_n[r] = sQs.n

        def ln_apply(r):
            rows = rw(r)
            p = r % 2
            j = r // 2
            if r >= 2:
                ve.wait_ge(sPE.h, tr_done[r - 2])
            sV.inc(ve.tensor_scalar(xn_tm[r % 2][0:rows, :],
                                    xload[r % 3][0:rows, :],
                                    stm[p][0:rows, j:j + 1],
                                    stv[p][0:rows, j:j + 1],
                                    ALU.subtract, ALU.mult))
            apply_done[r] = sV.n

        def ln_waves(r):
            rows = rw(r)
            col = 0 if r == 0 else 3 + (r - 1) * 128
            for wv in range(8):
                te.wait_ge(sV.h, apply_done[r] if wv == 0 else sV.n)
                for jj in range(2):
                    jch = wv * 2 + jj
                    te.transpose(pt[jj][:, 0:rows],
                                 xn_tm[r % 2][0:rows, jch * 128:(jch + 1) * 128],
                                 c_ident[0:rows, 0:rows])
                sPE.inc(te.nop())
                ve.wait_ge(sPE.h, sPE.n)
                for jj in range(2):
                    jch = wv * 2 + jj
                    sV.inc(ve.tensor_copy(
                        xn_fm[:, jch * PITCH + col: jch * PITCH + col + rows],
                        pt[jj][:, 0:rows]))
                te.wait_ge(sV.h, sV.n)
            tr_done[r] = sPE.n

        def ln_sqrt(r):
            rows = rw(r)
            p = r % 2
            j = r // 2
            se.wait_ge(sV.h, sqrt_gate[r])
            sS.inc(se.activation(stv[p][0:rows, j:j + 1],
                                 stv[p][0:rows, j:j + 1],
                                 AF.Sqrt, bias=c_eps[0:rows, 0:1]))
            sqrt_done[r] = sS.n

        def ln_recip(r):
            rows = rw(r)
            p = r % 2
            j = r // 2
            ve.wait_ge(sS.h, sqrt_done[r])
            ve.reciprocal(stv[p][0:rows, j:j + 1], stv[p][0:rows, j:j + 1])

        def ln_iter(r):
            rows = rw(r)
            p = r % 2
            j = r // 2
            xt = xload[r % 3][0:rows, :]
            ve.wait_ge(sQs.h, dma_n[r])
            ve.reduce_sum(stm[p][0:rows, j:j + 1], xt, AX.X)
            ve.scalar_tensor_tensor(sqd[0:rows, :], xt, 1.0, xt,
                                    ALU.bypass, ALU.mult)
            ve.reduce_sum(stv[p][0:rows, j:j + 1], sqd[0:rows, :], AX.X)
            ve.scalar_tensor_tensor(st2[p][0:rows, j:j + 1],
                                    stm[p][0:rows, j:j + 1],
                                    1.0 / (2048.0 * 2048.0),
                                    stm[p][0:rows, j:j + 1],
                                    ALU.mult, ALU.mult)
            ve.tensor_scalar_mul(stm[p][0:rows, j:j + 1],
                                 stm[p][0:rows, j:j + 1], 1.0 / D_MODEL)
            if r >= 1:
                ln_recip(r - 1)
            if r >= 2:
                ln_apply(r - 2)
            else:
                ve.scalar_tensor_tensor(sqd[0:rows, :], xt, 1.0, xt,
                                        ALU.bypass, ALU.mult)
            sV.inc(ve.scalar_tensor_tensor(
                stv[p][0:rows, j:j + 1], stv[p][0:rows, j:j + 1],
                1.0 / D_MODEL, st2[p][0:rows, j:j + 1],
                ALU.mult, ALU.subtract))
            var_done[r] = sV.n
            if r >= 2 and rw(r - 2) == 128:
                ln_waves(r - 2)
                sqrt_gate[r] = sV.n
            else:
                if r >= 2:
                    ln_waves(r - 2)
                sV.inc(ve.scalar_tensor_tensor(sqd[0:rows, :], xt, 1.0, xt,
                                               ALU.bypass, ALU.mult))
                sqrt_gate[r] = sV.n
            ln_sqrt(r)

        for r in range(9):
            ln_load(r)
            ln_iter(r)
        ln_recip(8)
        ln_apply(7)
        ln_waves(7)
        ln_apply(8)
        ln_waves(8)
        if spill_xn:
            ge.wait_ge(sV.h, sV.n)
            for jch in range(NDM):
                sQg.inc(ge.dma_start(
                    out=xn_hbm.ap()[jch],
                    in_=xn_fm[:, jch * PITCH + 3: jch * PITCH + 1027]), 16)

    def emit_inproj(d_chunks, spill_z):
        pe_done = {}
        cast_done = {}
        evac_done = {}
        zdma_done = {}
        conv_done = {}
        silu_done = {}
        nxb = 0
        nz = 0
        prev_xbc_i = -1
        for i, d in enumerate(d_chunks):
            par = i % 2
            if i >= 2:
                qe.wait_ge(sG.h, cast_done[i - 2][0])
                qe.wait_ge(sS.h, cast_done[i - 2][1])
            sQs.inc(qe.dma_start(out=wstg[par][:, :],
                                 in_=w_in.ap()[d].rearrange("p k m -> p (k m)")), 16)
            slab_n = sQs.n
            is_xbc = 32 <= d <= 65
            jx = d - 32
            if is_xbc:
                if nxb >= 2:
                    ve.wait_ge(sPE.h, conv_done[nxb - 2])
                for k in range(4):
                    sV.inc(ve.tensor_scalar_mul(
                        convd[nxb % 2][:, k * 128:(k + 1) * 128], c_ident[:, :],
                        c_convw[:, jx * 4 + k:jx * 4 + k + 1]))
            if i >= 2:
                ge.wait_ge(sPE.h, pe_done[i - 2])
                se.wait_ge(sPE.h, pe_done[i - 2])
            ge.wait_ge(sQs.h, slab_n)
            sG.inc(ge.tensor_copy(wbf[par][:, 0:1024], wstg[par][:, 0:1024]))
            se.wait_ge(sQs.h, slab_n)
            sS.inc(se.activation(wbf[par][:, 1024:2048], wstg[par][:, 1024:2048],
                                 AF.Copy))
            cast_done[i] = (sG.n, sS.n)
            te.wait_ge(sG.h, sG.n)
            te.wait_ge(sS.h, sS.n)
            if i >= 2:
                te.wait_ge(sV.h, evac_done[i - 2])
            pa, pb = ps[2 * par], ps[2 * par + 1]
            for k in range(NDM):
                lh = wbf[par][:, k * 128:(k + 1) * 128]
                te.matmul(pa[:, :], lh, xn_fm[:, k * PITCH + 3:k * PITCH + 515],
                          start=(k == 0), stop=(k == NDM - 1))
                te.matmul(pb[:, :], lh, xn_fm[:, k * PITCH + 515:k * PITCH + 1027],
                          start=(k == 0), stop=(k == NDM - 1))
            if is_xbc:
                if prev_xbc_i >= 0:
                    te.wait_ge(sV.h, evac_done[prev_xbc_i])
                    te.wait_ge(sS.h, silu_done[nxb - 1])
                for k in range(NDM):
                    te.matmul(ps_x[:, 0:3], wbf[par][:, k * 128:(k + 1) * 128],
                              xn_fm[:, k * PITCH:k * PITCH + 3],
                              start=(k == 0), stop=(k == NDM - 1))
            pe_done[i] = sPE.inc(te.nop())
            ve.wait_ge(sPE.h, sPE.n)
            if d <= 31:
                if nz >= 2 and spill_z:
                    ve.wait_ge(sQg.h, zdma_done[nz - 2])
                sV.inc(ve.tensor_copy(zstage[nz % 2][:, 0:512], pa[:, :]))
                sV.inc(ve.tensor_copy(zstage[nz % 2][:, 512:1024], pb[:, :]))
                evac_done[i] = sV.n
                if spill_z:
                    ge.wait_ge(sV.h, sV.n)
                    sQg.inc(ge.dma_start(out=z_hbm.ap()[d],
                                         in_=zstage[nz % 2][:, :]), 16)
                    zdma_done[nz] = sQg.n
                nz += 1
            elif d == 66:
                sV.inc(ve.tensor_copy(dt_raw[:, 0:512], pa[0:64, :]))
                sV.inc(ve.tensor_copy(dt_raw[:, 512:1024], pb[0:64, :]))
                evac_done[i] = sV.n
            else:
                st = xbst[nxb % 2]
                if nxb >= 2:
                    ve.wait_ge(sPE.h, conv_done[nxb - 2])
                sV.inc(ve.tensor_copy(st[:, 0:3], ps_x[:, 0:3]))
                sV.inc(ve.tensor_copy(st[:, 3:515], pa[:, :]))
                sV.inc(ve.tensor_copy(st[:, 515:1027], pb[:, :]))
                evac_done[i] = sV.n
                te.wait_ge(sV.h, sV.n)
                if nxb >= 1:
                    te.wait_ge(sS.h, silu_done[nxb - 1])
                for k in range(4):
                    cdg = convd[nxb % 2][:, k * 128:(k + 1) * 128]
                    te.matmul(ps[4][:, :], cdg, st[:, k:k + 512],
                              start=(k == 0), stop=(k == 3))
                    te.matmul(ps_x[:, :], cdg, st[:, 512 + k:512 + k + 512],
                              start=(k == 0), stop=(k == 3))
                conv_done[nxb] = sPE.inc(te.nop())
                se.wait_ge(sPE.h, sPE.n)
                if jx < 32:
                    dsts = (x_ssm[:, jx * 1024:jx * 1024 + 512],
                            x_ssm[:, jx * 1024 + 512:(jx + 1) * 1024])
                elif jx == 32:
                    dsts = (Bfm[:, 0:512], Bfm[:, 512:1024])
                else:
                    dsts = (Cfm[:, 0:512], Cfm[:, 512:1024])
                sS.inc(se.activation(dsts[0], ps[4][:, :], AF.Silu,
                                     bias=c_convb[:, jx:jx + 1]))
                sS.inc(se.activation(dsts[1], ps_x[:, :], AF.Silu,
                                     bias=c_convb[:, jx:jx + 1]))
                silu_done[nxb] = sS.n
                prev_xbc_i = i
                nxb += 1

    def emit_dt_cs(c):
        sl = slice(c * CHUNK, (c + 1) * CHUNK)
        se.wait_ge(sV.h, sV.n)
        sS.inc(se.activation(dtc[:, :], dt_raw[:, sl], AF.Exp, bias=c_dtb[:, :]))
        sS.inc(se.activation(dtc[:, :], dtc[:, :], AF.Ln, bias=c_one[0:64, 0:1]))
        ve.wait_ge(sS.h, sS.n)
        if "cs_w" in ST:
            ve.wait_ge(sQs.h, ST["cs_w"])
        sV.inc(ve.tensor_scalar_mul(csc[:, :], dtc[:, :], c_aneg[:, :]))
        sV.inc(ve.tensor_tensor_scan(csc[:, :], csc[:, :], dtc[:, :],
                                     c_zero[0:64, 0:1], ALU.add, ALU.bypass))
        sV.inc(ve.tensor_copy(cshi[:, :], csc[:, :]))
        sV.inc(ve.tensor_sub(cslo[:, :], csc[:, :], cshi[:, :]))
        sV.inc(ve.tensor_copy(dtb16[:, :], dtc[:, :]))
        for stile in range(2):
            te.wait_ge(sV.h, sV.n)
            te.transpose(pt[0][:, 0:64], cshi[:, stile * 128:(stile + 1) * 128],
                         c_ident[0:64, 0:64])
            te.transpose(pt[1][:, 0:64], cslo[:, stile * 128:(stile + 1) * 128],
                         c_ident[0:64, 0:64])
            sPE.inc(te.nop())
            ve.wait_ge(sPE.h, sPE.n)
            ve.tensor_copy(cstT[:, stile * 64:(stile + 1) * 64], pt[0][:, 0:64])
            sV.inc(ve.tensor_add(cstT[:, stile * 64:(stile + 1) * 64],
                                 cstT[:, stile * 64:(stile + 1) * 64],
                                 pt[1][:, 0:64]))
            te.wait_ge(sV.h, sV.n)
            te.transpose(pt[0][:, 0:64], dtb16[:, stile * 128:(stile + 1) * 128],
                         c_ident[0:64, 0:64])
            sPE.inc(te.nop())
            ve.wait_ge(sPE.h, sPE.n)
            sV.inc(ve.tensor_copy(dtT[:, stile * 64:(stile + 1) * 64],
                                  pt[0][:, 0:64]))
            te.wait_ge(sV.h, sV.n)
        se.wait_ge(sV.h, sV.n)
        sS.inc(se.activation(dec[:, :], csc[:, :], AF.Exp, scale=-1.0,
                             bias=csc[:, CHUNK - 1:CHUNK]))
        sS.inc(se.activation(cdcol[:, :], csc[:, CHUNK - 1:CHUNK], AF.Exp))
        ve.wait_ge(sS.h, sS.n)
        sV.inc(ve.tensor_copy(decb16[:, :], dec[:, :]))
        for stile in range(2):
            te.wait_ge(sV.h, sV.n)
            te.transpose(pt[1][:, 0:64], decb16[:, stile * 128:(stile + 1) * 128],
                         c_ident[0:64, 0:64])
            sPE.inc(te.nop())
            ve.wait_ge(sPE.h, sPE.n)
            sV.inc(ve.tensor_copy(decT[:, stile * 64:(stile + 1) * 64],
                                  pt[1][:, 0:64]))
            te.wait_ge(sV.h, sV.n)
        sV.inc(ve.tensor_scalar_mul(cdd[:, :], c_ident[0:64, 0:64], cdcol[:, :]))
        te.wait_ge(sV.h, sV.n)
        te.matmul(ps_x[:, 0:64], c_ones64[:, :], cdd[:, :], start=True, stop=True)
        sPE.inc(te.nop())
        ve.wait_ge(sPE.h, sPE.n)
        sV.inc(ve.tensor_copy(dbc[:, :], ps_x[:, 0:64]))
        qe.wait_ge(sV.h, sV.n)
        sQs.inc(qe.dma_start(out=cs_hbm.ap(), in_=csc[:, :]), 16)
        ST["cs_w"] = sQs.n

    def emit_xdt(c):
        for stile in range(2):
            for wv in range(16):
                te.wait_ge(sV.h, sV.n)
                for jj in range(2):
                    jch = wv * 2 + jj
                    te.transpose(pt[jj][:, 0:128],
                                 x_ssm[:, jch * 1024 + c * CHUNK + stile * 128:
                                       jch * 1024 + c * CHUNK + (stile + 1) * 128],
                                 c_ident[:, :])
                sPE.inc(te.nop())
                ve.wait_ge(sPE.h, sPE.n)
                for jj in range(2):
                    jch = wv * 2 + jj
                    dsl = dtT.ap()[:, stile * 64 + 2 * jch: stile * 64 + 2 * jch + 2]
                    in1 = bass.AP(dsl.tensor, dsl.offset,
                                  [list(dsl.ap[0]), list(dsl.ap[1]), [0, 64]])
                    sV.inc(ve.scalar_tensor_tensor(
                        Xdt[:, stile * D_INNER + jch * 128:
                            stile * D_INNER + (jch + 1) * 128]
                        .rearrange("p (h e) -> p h e", h=2),
                        pt[jj][:, 0:128]
                        .rearrange("p (h e) -> p h e", h=2),
                        1.0, in1, ALU.bypass, ALU.mult))
                te.wait_ge(sV.h, sV.n)
            te.transpose(pt[0][:, 0:128],
                         Bfm[:, c * CHUNK + stile * 128:c * CHUNK + (stile + 1) * 128],
                         c_ident[:, :])
            sPE.inc(te.nop())
            ve.wait_ge(sPE.h, sPE.n)
            sV.inc(ve.tensor_copy(Btm[:, stile * 128:(stile + 1) * 128],
                                  pt[0][:, 0:128]))
            te.wait_ge(sV.h, sV.n)

    def emit_states(c):
        ve.wait_ge(sPE.h, sPE.n)
        for stile in range(2):
            dsl = decT.ap()[:, stile * 64:stile * 64 + 64]
            in1 = bass.AP(dsl.tensor, dsl.offset,
                          [list(dsl.ap[0]), list(dsl.ap[1]), [0, 64]])
            sV.inc(ve.scalar_tensor_tensor(
                Xdt[:, stile * D_INNER:(stile + 1) * D_INNER]
                .rearrange("p (h e) -> p h e", h=64),
                Xdt[:, stile * D_INNER:(stile + 1) * D_INNER]
                .rearrange("p (h e) -> p h e", h=64),
                1.0, in1, ALU.bypass, ALU.mult))
        upd = {}
        for blk in range(8):
            pb = ps[0]
            te.wait_ge(sV.h, sV.n)
            if blk >= 1:
                te.wait_ge(sV.h, upd[blk - 1])
            for stile in range(2):
                te.matmul(pb[:, :], Btm[:, stile * 128:(stile + 1) * 128],
                          Xdt[:, stile * D_INNER + blk * 512:
                              stile * D_INNER + (blk + 1) * 512],
                          start=(stile == 0), stop=(stile == 1))
            sPE.inc(te.nop())
            ve.wait_ge(sPE.h, sPE.n)
            dsl = dbc.ap()[:, blk * 8:blk * 8 + 8]
            in1 = bass.AP(dsl.tensor, dsl.offset,
                          [list(dsl.ap[0]), list(dsl.ap[1]), [0, 64]])
            sV.inc(ve.scalar_tensor_tensor(
                state[:, blk * 512:(blk + 1) * 512].rearrange("p (h e) -> p h e", h=8),
                state[:, blk * 512:(blk + 1) * 512].rearrange("p (h e) -> p h e", h=8),
                1.0, in1, ALU.bypass, ALU.mult))
            sV.inc(ve.tensor_add(state[:, blk * 512:(blk + 1) * 512],
                                 state[:, blk * 512:(blk + 1) * 512], pb[:, :]))
            upd[blk] = sV.n

    def emit_scan_main(c):
        if c > 0:
            ge.wait_ge(sV.h, ST["yg_last"])
        sQg.inc(ge.dma_start(
            out=szbuf[:, :],
            in_=z_hbm.ap()[:, :, c * CHUNK:(c + 1) * CHUNK].transpose([1, 0, 2])), 16)
        se.wait_ge(sQg.h, sQg.n)
        sS.inc(se.activation(szbuf[:, :], szbuf[:, :], AF.Silu))
        sz_done = sS.n

        emit_dt_cs(c)
        emit_xdt(c)

        te.wait_ge(sV.h, sV.n)
        gps = (ps_x, ps[4])
        for stile in range(2):
            te.matmul(gps[stile][:, 0:256],
                      Bfm[:, c * CHUNK + stile * 128:c * CHUNK + (stile + 1) * 128],
                      Cfm[:, c * CHUNK:(c + 1) * CHUNK],
                      start=True, stop=True)
        sPE.inc(te.nop())
        ve.wait_ge(sPE.h, sPE.n)
        for stile in range(2):
            sV.inc(ve.tensor_mul(Gm[:, stile * 256:(stile + 1) * 256],
                                 gps[stile][:, 0:256],
                                 c_mask[:, stile * 256:(stile + 1) * 256]))
        gm_done = sV.n

        ypair = 0
        y_evac = {}
        cexp_free = {}
        for g in range(8):
            if g >= 1:
                qe.wait_ge(sV.h, ST["t_last"])
                qe.wait_ge(sS.h, ST["exprb_s"])
            qe.wait_ge(sQs.h, ST["cs_w"])
            src = bass.AP(cs_hbm, g * 8 * CHUNK, [[0, 128], [1, 2048]])
            sQs.inc(qe.dma_start(out=RBb[:, :], in_=src), 16)
            rb_n = sQs.n
            se.wait_ge(sQs.h, rb_n)
            if g >= 1:
                se.wait_ge(sV.h, ST["cexp_v"])
            sS.inc(se.activation(ExpRB[:, :], RBb[:, :], AF.Exp))
            ST["exprb_s"] = sS.n
            ve.wait_ge(sS.h, sS.n)
            if g >= 1:
                ve.wait_ge(sPE.h, cexp_free[g - 1])
            cslice = Cfm.ap()[:, c * CHUNK:(c + 1) * CHUNK]
            sV.inc(ve.tensor_mul(
                Cexp[0][:, :].rearrange("p (h c2) -> p h c2", h=8),
                bass.AP(cslice.tensor, cslice.offset,
                        [list(cslice.ap[0]), [0, 8], list(cslice.ap[1])]),
                ExpRB[:, :].rearrange("p (h c2) -> p h c2", h=8)))
            ST["cexp_v"] = sV.n
            for stile in range(2):
                ve.wait_ge(sQs.h, rb_n)
                ve.wait_ge(sS.h, ST.get(f"e0_{stile}", 0))
                for hh in range(8):
                    sV.inc(ve.tensor_scalar(
                        Tbuf[:, hh * 256:(hh + 1) * 256],
                        RBb[:, hh * 256:(hh + 1) * 256],
                        cstT[:, stile * 64 + g * 8 + hh: stile * 64 + g * 8 + hh + 1],
                        0.0, ALU.subtract, ALU.min))
                tpass = sV.n
                se.wait_ge(sV.h, tpass)
                if g >= 1:
                    se.wait_ge(sPE.h, ST["ymm_g"])
                sS.inc(se.activation(Ebuf[:, stile * 2048:(stile + 1) * 2048],
                                     Tbuf[:, :], AF.Exp))
                ST[f"e0_{stile}"] = sS.n
                ve.wait_ge(sS.h, sS.n)
                ve.wait_ge(sV.h, gm_done)
                gsl = Gm.ap()[:, stile * 256:(stile + 1) * 256]
                sV.inc(ve.tensor_mul(
                    Ebuf[:, stile * 2048:(stile + 1) * 2048]
                    .rearrange("p (h c2) -> p h c2", h=8),
                    Ebuf[:, stile * 2048:(stile + 1) * 2048]
                    .rearrange("p (h c2) -> p h c2", h=8),
                    bass.AP(gsl.tensor, gsl.offset,
                            [list(gsl.ap[0]), [0, 8], list(gsl.ap[1])])))
                ST[f"e_{stile}"] = sV.n
            ST["t_last"] = sV.n
            te.wait_ge(sV.h, ST["e_1"])
            for pr in range(4):
                hp = g * 4 + pr
                h0 = g * 8 + pr * 2
                if ypair >= 2:
                    ve.wait_ge(sPE.h, ST.get(f"ymm_p{ypair-2}", 0))
                sV.inc(ve.tensor_scalar_mul(ddtile[ypair % 2][:, :], c_ident[:, :],
                                            c_drep[:, hp:hp + 1]))
                sV.inc(ve.tensor_copy(stcast[ypair % 2][:, :],
                                      state[:, h0 * 64:(h0 + 2) * 64]))
                dd_v = sV.n
                pA = ps[1 + 2 * (ypair % 2)]
                pB = ps[2 + 2 * (ypair % 2)]
                if ypair >= 2:
                    te.wait_ge(sV.h, y_evac[ypair - 2])
                te.wait_ge(sV.h, dd_v)
                pYs = (pA[0:64, 0:256], pB[64:128, 0:256])
                for colg in range(2):
                    for stile in range(2):
                        h = h0 + colg
                        te.matmul(pYs[colg],
                                  Xdt[:, stile * D_INNER + h * 64:
                                      stile * D_INNER + (h + 1) * 64],
                                  Ebuf[:, stile * 2048 + (pr * 2 + colg) * 256:
                                       stile * 2048 + (pr * 2 + colg + 1) * 256],
                                  start=(stile == 0), stop=False,
                                  tile_position=(0, colg * 64))
                xs = x_ssm[:, hp * 1024 + c * CHUNK:hp * 1024 + (c + 1) * CHUNK]
                te.matmul(pYs[0], ddtile[ypair % 2][:, 0:64], xs,
                          start=False, stop=False, tile_position=(0, 0))
                te.matmul(pYs[1], ddtile[ypair % 2][:, 64:128], xs,
                          start=False, stop=False, tile_position=(0, 64))
                for colg in range(2):
                    te.matmul(pYs[colg],
                              stcast[ypair % 2][:, colg * 64:(colg + 1) * 64],
                              Cexp[0][:, (pr * 2 + colg) * 256:
                                          (pr * 2 + colg + 1) * 256],
                              start=False, stop=True,
                              tile_position=(0, colg * 64))
                ST[f"ymm_p{ypair}"] = sPE.inc(te.nop())
                ve.wait_ge(sPE.h, sPE.n)
                ve.wait_ge(sS.h, sz_done)
                sV.inc(ve.tensor_mul(
                    x_ssm[0:64, hp * 1024 + c * CHUNK:hp * 1024 + (c + 1) * CHUNK],
                    pYs[0], szbuf[0:64, hp * 256:(hp + 1) * 256]))
                sV.inc(ve.tensor_mul(
                    x_ssm[64:128, hp * 1024 + c * CHUNK:hp * 1024 + (c + 1) * CHUNK],
                    pYs[1], szbuf[64:128, hp * 256:(hp + 1) * 256]))
                y_evac[ypair] = sV.n
                ypair += 1
            ST["ymm_g"] = sPE.n
            cexp_free[g] = sPE.n
        ST["yg_last"] = sV.n

        for jj in range(NFI):
            ve.wait_ge(sV.h, y_evac[jj])
            if jj > 0:
                ve.wait_ge(sPE.h, sPE.n)
            sV.inc(ve.tensor_mul(
                ySq[:, :],
                x_ssm[:, jj * 1024 + c * CHUNK:jj * 1024 + (c + 1) * CHUNK],
                x_ssm[:, jj * 1024 + c * CHUNK:jj * 1024 + (c + 1) * CHUNK]))
            te.wait_ge(sV.h, sV.n)
            te.matmul(ps_x[0:1, 0:256], c_onescb[:, :], ySq[:, :],
                      start=(jj == 0), stop=(jj == NFI - 1))
            sPE.inc(te.nop())
        se.wait_ge(sPE.h, sPE.n)
        sS.inc(se.activation(rsq_r[0:1, :], ps_x[0:1, 0:256], AF.Sqrt,
                             scale=1.0 / D_INNER, bias=c_eps[0:1, 0:1]))
        ve.wait_ge(sS.h, sS.n)
        sV.inc(ve.reciprocal(rsq_r[0:1, :], rsq_r[0:1, :]))
        te.wait_ge(sV.h, sV.n)
        te.matmul(ps[0][:, 0:256], c_onesf[:, :], rsq_r[0:1, :],
                  start=True, stop=True)
        sPE.inc(te.nop())
        ve.wait_ge(sPE.h, sPE.n)
        for jj in range(NFI):
            sV.inc(ve.scalar_tensor_tensor(
                x_ssm[:, jj * 1024 + c * CHUNK:jj * 1024 + (c + 1) * CHUNK],
                x_ssm[:, jj * 1024 + c * CHUNK:jj * 1024 + (c + 1) * CHUNK],
                c_nw[:, jj:jj + 1], ps[0][:, 0:256], ALU.mult, ALU.mult))
        ST["yn_done"] = sV.n

        emit_states(c)

    emit_ln(x_carry, zeros3, spill_xn=False)
    bar()
    if ckpt("LNc"):
        ctx.close()
        return nc
    emit_inproj(list(range(32, 67)), spill_z=False)
    bar()
    ve.memset(state[:, :], 0.0)
    sV.inc(ve.nop())
    for c in range(NCHUNK):
        emit_dt_cs(c)
        emit_xdt(c)
        emit_states(c)
    ve.wait_ge(sV.h, sV.n)
    sV.inc(ve.tensor_scalar_mul(state[:, :], state[:, :], c_gate[:, :]))
    bar()
    if ckpt("SCc"):
        ctx.close()
        return nc
    if tap is not None and tap[0] == "carry_state":
        sQs.inc(qe.dma_start(out=tap_t.ap(), in_=state[:, :]), 16)
        qe.wait_ge(sQs.h, sQs.n)
        bar()

    emit_ln(x_rows, x_prev3, spill_xn=True)
    bar()
    if ckpt("LN"):
        ctx.close()
        return nc
    tap_now("xnc0", xn_fm[:, 3:1027])
    tap_now("xn", xn_fm[:, 0:PITCH * 2])
    emit_inproj(list(range(0, 67)), spill_z=True)
    bar()
    if ckpt("IP"):
        ctx.close()
        return nc
    tap_now("xssm", x_ssm[:, 0:4096])
    tap_now("bfm", Bfm[:, :])
    tap_now("cfm", Cfm[:, :])
    tap_now("dtraw", dt_raw[:, :])
    for c in range(NCHUNK):
        emit_scan_main(c)
        if tap is not None and tap[0] == f"state{c}":
            ve.wait_ge(sV.h, sV.n)
            sV.inc(ve.nop())
            qe.wait_ge(sV.h, sV.n)
            sQs.inc(qe.dma_start(out=tap_t.ap(), in_=state[:, :]), 16)
            qe.wait_ge(sQs.h, sQs.n)
    bar()
    if ckpt("SC"):
        ctx.close()
        return nc
    tap_now("yn", x_ssm[:, 0:4096])

    nw_i = 0
    op_done = {}
    m4_evac = 0
    for nb in range(4):
        for wave in range(2):
            for k in range(NFI):
                par = nw_i % 2
                if nw_i >= 2:
                    qe.wait_ge(sG.h, op_done[nw_i - 2][0])
                    qe.wait_ge(sS.h, op_done[nw_i - 2][1])
                sQs.inc(qe.dma_start(out=wstg[par][:, 0:512],
                                     in_=w_out.ap()[k][:, nb * 512:(nb + 1) * 512]), 16)
                slab_n = sQs.n
                if nw_i >= 2:
                    ge.wait_ge(sPE.h, op_done[nw_i - 2][2])
                    se.wait_ge(sPE.h, op_done[nw_i - 2][2])
                ge.wait_ge(sQs.h, slab_n)
                sG.inc(ge.tensor_copy(wbf[par][:, 0:256], wstg[par][:, 0:256]))
                se.wait_ge(sQs.h, slab_n)
                sS.inc(se.activation(wbf[par][:, 256:512], wstg[par][:, 256:512],
                                     AF.Copy))
                sG.inc(ge.tensor_copy(convd[0][:, :], wstg[par][:, 0:512]))
                sS.inc(se.activation(convd[1][:, :], wstg[par][:, 0:512],
                                     AF.Copy))
                te.wait_ge(sG.h, sG.n)
                te.wait_ge(sS.h, sS.n)
                if k == 0:
                    te.wait_ge(sV.h, m4_evac)
                for tb in range(4):
                    tbg = wave * 4 + tb
                    te.matmul(ps[tb][:, :],
                              x_ssm[:, k * 1024 + tbg * 128:k * 1024 + (tbg + 1) * 128],
                              wbf[par][:, 0:512],
                              start=(k == 0), stop=(k == NFI - 1))
                op_done[nw_i] = (sG.n, sS.n, sPE.inc(te.nop()))
                nw_i += 1
            ve.wait_ge(sPE.h, sPE.n)
            for tb in range(4):
                tbg = wave * 4 + tb
                ve.wait_ge(sQg.h, ST.get(f"sdma_{(nb, wave, tb)}", 0))
                sV.inc(ve.tensor_copy(s_stage[tb % 2][:, :], ps[tb][:, :]))
                ge.wait_ge(sV.h, sV.n)
                sQg.inc(ge.dma_start(out=s_hbm.ap()[tbg][:, nb * 512:(nb + 1) * 512],
                                     in_=s_stage[tb % 2][:, :]), 16)
                nxt = (nb, wave, tb + 2) if tb < 2 else (
                    (nb, wave + 1, tb - 2) if wave == 0 else (nb + 1, 0, tb - 2))
                ST[f"sdma_{nxt}"] = sQg.n
            m4_evac = sV.n
    bar()
    if ckpt("OP"):
        ctx.close()
        return nc
    if tap is not None and tap[0] == "s":
        sQs.inc(qe.dma_start(out=tap_t.ap(), in_=s_hbm.ap()), 16)
        qe.wait_ge(sQs.h, sQs.n)
        bar()

    hdone = {}
    h12_done = {}
    for a in range(NDM):
        if a >= 2:
            ve.wait_ge(sPE.h, h12_done[a - 2])
        sV.inc(ve.tensor_scalar_mul(hbtile[a % 2][:, :], c_h128[:, :],
                                    c_suug[:, a:a + 1]))
        hp_v = sV.n
        if a >= 2:
            qe.wait_ge(sV.h, hdone[a - 2])
        sQs.inc(qe.dma_start(out=xnld[a % 2][:, :], in_=xn_hbm.ap()[a]), 16)
        te.wait_ge(sQs.h, sQs.n)
        te.wait_ge(sV.h, hp_v)
        if a >= 2:
            te.wait_ge(sQg.h, ST.get(f"perm1_{a-2}", 0))
            te.wait_ge(sQs.h, ST.get(f"perm1q_{a-2}", 0))
            te.wait_ge(sV.h, hdone[a - 2])
        pa, pb = ps[2 * (a % 2)], ps[2 * (a % 2) + 1]
        te.matmul(pa[:, :], hbtile[a % 2][:, :], xnld[a % 2][:, 0:512],
                  start=True, stop=True)
        te.matmul(pb[:, :], hbtile[a % 2][:, :], xnld[a % 2][:, 512:1024],
                  start=True, stop=True)
        h12_done[a] = sPE.inc(te.nop())
        ve.wait_ge(sPE.h, sPE.n)
        sV.inc(ve.tensor_copy(v1a[a % 2][:, 0:512], pa[:, :]))
        sV.inc(ve.tensor_copy(v1a[a % 2][:, 512:1024], pb[:, :]))
        hdone[a] = sV.n
        ge.wait_ge(sV.h, sV.n)
        qe.wait_ge(sV.h, sV.n)
        for i8 in range(8):
            dst = (h1perm[16 * i8 + a:16 * i8 + a + 1, :]
                   .rearrange("p (j t2) -> p j t2", j=16))
            src = v1a[a % 2][16 * i8:16 * (i8 + 1), :]
            if i8 % 2 == 0:
                sQg.inc(ge.dma_start(out=dst, in_=src), 16)
            else:
                sQs.inc(qe.dma_start(out=dst, in_=src), 16)
        ST[f"perm1_{a}"] = sQg.n
        ST[f"perm1q_{a}"] = sQs.n
    ge.wait_ge(sQg.h, sQg.n)
    qe.wait_ge(sQs.h, sQs.n)
    bar()
    tap_now("h1pre", h1perm[:, 0:2048])
    for jv in range(16):
        pa = ps[jv % 4]
        if jv >= 4:
            te.wait_ge(sV.h, hdone[100 + jv - 4])
        for th in range(2):
            if th == 1:
                te.wait_ge(sV.h, sV.n)
            te.matmul(pa[:, :], c_h16[:, :],
                      h1perm[:, jv * 1024 + th * 512:jv * 1024 + (th + 1) * 512],
                      start=True, stop=True)
            sPE.inc(te.nop())
            ve.wait_ge(sPE.h, sPE.n)
            sV.inc(ve.tensor_copy(
                h1perm[:, jv * 1024 + th * 512:jv * 1024 + (th + 1) * 512],
                pa[:, :]))
        hdone[100 + jv] = sV.n
    bar()
    if ckpt("F1"):
        ctx.close()
        return nc
    tap_now("h1", h1perm[:, 0:2048])

    for a in range(NA):
        if a >= 2:
            ve.wait_ge(sPE.h, ST.get(f"f2hb_{a-2}_1", 0))
        sV.inc(ve.tensor_scalar_mul(hbtile[a % 2][:, :], c_h128[:, :],
                                    hcol[:, a:a + 1]))
        hb_v = sV.n
        if a >= 1:
            qe.wait_ge(sG.h, ST[f"f2cast_{a-1}"][0])
        sQs.inc(qe.dma_start(out=wstg[0][:, :],
                             in_=w_up.ap()[a].rearrange("p k m -> p (k m)")), 16)
        up_n = sQs.n
        sQa.inc(se.dma_start(out=wstg[1][:, :],
                             in_=w_gate.ap()[a].rearrange("p k m -> p (k m)")), 16)
        gt_n = sQa.n
        if a >= 1:
            ge.wait_ge(sPE.h, ST[f"f2pe_{a-1}"])
            se.wait_ge(sPE.h, ST[f"f2pe_{a-1}"])
        ge.wait_ge(sQs.h, up_n)
        sG.inc(ge.tensor_copy(wbf[0][:, :], wstg[0][:, :]))
        se.wait_ge(sQa.h, gt_n)
        sS.inc(se.activation(wbf[1][:, :], wstg[1][:, :], AF.Copy))
        ST[f"f2cast_{a}"] = (sG.n, sS.n)
        te.wait_ge(sG.h, sG.n)
        te.wait_ge(sS.h, sS.n)
        te.wait_ge(sV.h, hb_v)
        for th in range(2):
            pu, pg = ps[2 * (a % 2)], ps[2 * (a % 2) + 1]
            if a >= 2:
                te.wait_ge(sV.h, ST.get(f"f2v_{a-2}_{th}", 0))
            for k in range(NDM):
                te.matmul(pu[:, :], wbf[0][:, k * 128:(k + 1) * 128],
                          h1perm[:, k * 1024 + th * 512:k * 1024 + (th + 1) * 512],
                          start=(k == 0), stop=(k == NDM - 1))
                te.matmul(pg[:, :], wbf[1][:, k * 128:(k + 1) * 128],
                          h1perm[:, k * 1024 + th * 512:k * 1024 + (th + 1) * 512],
                          start=(k == 0), stop=(k == NDM - 1))
            mm_n = sPE.inc(te.nop())
            se.wait_ge(sPE.h, mm_n)
            if a >= 1:
                se.wait_ge(sV.h, ST.get(f"f2v_{a-1}_{th}", 0))
            sS.inc(se.activation(sgbuf[th][:, :], pg[:, :], AF.Silu,
                                 scale=c_wscg[:, a:a + 1]))
            ve.wait_ge(sS.h, sS.n)
            if a >= 1:
                ve.wait_ge(sPE.h, ST.get(f"f2hb_{a-1}_{th}", 0))
            sV.inc(ve.tensor_mul(vbuf[th][:, :], pu[:, :], sgbuf[th][:, :]))
            ST[f"f2v_{a}_{th}"] = sV.n
            te.wait_ge(sV.h, sV.n)
            if a >= 1:
                te.wait_ge(sV.h, ST.get(f"f2e_{a-1}_{th}", 0))
            hbps = ps[4] if th == 0 else ps_x
            te.matmul(hbps[:, :], hbtile[a % 2][:, :],
                      vbuf[th][:, :], start=True, stop=True)
            ST[f"f2hb_{a}_{th}"] = sPE.inc(te.nop())
            ve.wait_ge(sPE.h, sPE.n)
            if a >= 1:
                ve.wait_ge(sQg.h, ST.get(f"perm2_{a-1}_{th}", 0))
            sV.inc(ve.tensor_copy(vbuf[th][:, :], hbps[:, :]))
            ST[f"f2e_{a}_{th}"] = sV.n
            ge.wait_ge(sV.h, sV.n)
            for p2 in range(2):
                sQg.inc(ge.dma_start(
                    out=bass.AP(H2buf.ap().tensor,
                                H2buf.ap().offset + (64 * p2 + a) * 65536 + th * 512,
                                [[65536, 1], [1024, 64], [1, 512]]),
                    in_=vbuf[th][64 * p2:64 * (p2 + 1), :]), 16)
            ST[f"perm2_{a}_{th}"] = sQg.n
        ST[f"f2pe_{a}"] = sPE.n
    ge.wait_ge(sQg.h, sQg.n)
    bar()
    if ckpt("F2"):
        ctx.close()
        return nc

    h64_ev = {}
    h64_mm = {}
    for gch in range(NA):
        for tblk in range(2):
            idx = gch * 2 + tblk
            pa = (ps[0], ps[1], ps[2], ps[3])[idx % 4]
            if idx >= 4:
                te.wait_ge(sV.h, h64_ev[idx - 4])
            te.matmul(pa[:, :], c_h64[:, :],
                      H2buf[:, gch * 1024 + tblk * 512:gch * 1024 + (tblk + 1) * 512],
                      start=True, stop=True)
            h64_mm[idx] = sPE.inc(te.nop())
            ve.wait_ge(sPE.h, h64_mm[idx])
            sV.inc(ve.tensor_copy(
                H2buf[:, gch * 1024 + tblk * 512:gch * 1024 + (tblk + 1) * 512],
                pa[:, :]))
            h64_ev[idx] = sV.n
    bar()
    if ckpt("F3"):
        ctx.close()
        return nc
    tap_now("h2", H2buf[:, 0:2048])

    for i4 in range(4):
        if i4 > 0:
            qe.wait_ge(sPE.h, sPE.n)
        sQs.inc(qe.dma_start(out=wdp_row[0:1, :], in_=wsc_dp.ap()[i4:i4 + 1, :]), 16)
        te.wait_ge(sQs.h, sQs.n)
        if i4 > 0:
            te.wait_ge(sV.h, sV.n)
        te.matmul(ps_x[:, 0:512], c_onesf[:, :], wdp_row[0:1, :],
                  start=True, stop=True)
        sPE.inc(te.nop())
        ve.wait_ge(sPE.h, sPE.n)
        sV.inc(ve.tensor_scalar(wscRB[:, i4 * 512:(i4 + 1) * 512],
                                ps_x[:, 0:512], SC8192, None, ALU.mult))
    bar()

    nw_i = 0
    f4_done = {}
    f4_evac = 0
    piece_hist = []
    for nb in range(4):
        for th in range(2):
            for gch in range(NA):
                par = nw_i % 2
                if nw_i >= 2:
                    qe.wait_ge(sG.h, f4_done[nw_i - 2][0])
                    qe.wait_ge(sS.h, f4_done[nw_i - 2][1])
                sQs.inc(qe.dma_start(
                    out=wstg[par][:, 0:512],
                    in_=w_down.ap()[gch][:, nb * 512:(nb + 1) * 512]), 16)
                slab_n = sQs.n
                if nw_i >= 2:
                    ge.wait_ge(sPE.h, f4_done[nw_i - 2][2])
                    se.wait_ge(sPE.h, f4_done[nw_i - 2][2])
                ge.wait_ge(sQs.h, slab_n)
                sG.inc(ge.tensor_copy(wbf[par][:, 0:256], wstg[par][:, 0:256]))
                se.wait_ge(sQs.h, slab_n)
                sS.inc(se.activation(wbf[par][:, 256:512], wstg[par][:, 256:512],
                                     AF.Copy))
                sG.inc(ge.tensor_copy(convd[0][:, :], wstg[par][:, 0:512]))
                sS.inc(se.activation(convd[1][:, :], wstg[par][:, 0:512],
                                     AF.Copy))
                te.wait_ge(sG.h, sG.n)
                te.wait_ge(sS.h, sS.n)
                if gch == 0:
                    te.wait_ge(sV.h, f4_evac)
                for tb in range(4):
                    te.matmul(ps[tb][:, :],
                              H2buf[:, gch * 1024 + th * 512 + tb * 128:
                                    gch * 1024 + th * 512 + (tb + 1) * 128],
                              wbf[par][:, 0:512],
                              start=(gch == 0), stop=(gch == NA - 1))
                f4_done[nw_i] = (sG.n, sS.n, sPE.inc(te.nop()))
                nw_i += 1
            for tb in range(4):
                tbg = th * 4 + tb
                pi = tb % 2
                prev = piece_hist[-2] if len(piece_hist) >= 2 else 0
                se.wait_ge(sV.h, prev)
                sQa.inc(se.dma_start(
                    out=sp_b[pi][:, :],
                    in_=s_hbm.ap()[tbg][:, nb * 512:(nb + 1) * 512]), 16)
                qe.wait_ge(sV.h, prev)
                sQs.inc(qe.dma_start(
                    out=xpf[pi][:, :],
                    in_=x_rows.ap()[tbg * 128:(tbg + 1) * 128,
                                    nb * 512:(nb + 1) * 512]), 16)
                ve.wait_ge(sPE.h, sPE.n)
                ve.wait_ge(sQa.h, sQa.n)
                ve.wait_ge(sQs.h, sQs.n)
                ve.wait_ge(sQg.h, sQg.n)
                sV.inc(ve.tensor_mul(gp_f[:, :], ps[tb][:, :],
                                     wscRB[:, nb * 512:(nb + 1) * 512]))
                se.wait_ge(sV.h, sV.n)
                sS.inc(se.activation(gp_b[pi][:, :], gp_f[:, :], AF.Silu))
                ve.wait_ge(sS.h, sS.n)
                sV.inc(ve.tensor_mul(gp_b[pi][:, :], gp_b[pi][:, :], sp_b[pi][:, :]))
                sV.inc(ve.tensor_add(opf[pi][:, :], xpf[pi][:, :], gp_b[pi][:, :]))
                piece_hist.append(sV.n)
                ge.wait_ge(sV.h, sV.n)
                sQg.inc(ge.dma_start(
                    out=out_t.ap()[tbg * 128:(tbg + 1) * 128,
                                   nb * 512:(nb + 1) * 512],
                    in_=opf[pi][:, :]), 16)
            f4_evac = sV.n
    ge.wait_ge(sQg.h, sQg.n)
    qe.wait_ge(sQg.h, sQg.n)

    ctx.close()
    return nc


def kernel(**inputs):
    shared = _stage_shared(inputs)
    in_maps = []
    for core in range(8):
        m = dict(shared)
        m.update(_stage_core(inputs, core))
        in_maps.append(m)
    nc = build_nc()
    res = run_bass_kernel_spmd(nc, in_maps, core_ids=list(range(8)))
    out = np.empty((4, 2048, D_MODEL), np.float32)
    for core in range(8):
        b_, half = core // 2, core % 2
        out[b_, half * T:(half + 1) * T, :] = res.results[core]["out"]
    return out
